# version 18
# baseline (speedup 1.0000x reference)
import math
from contextlib import ExitStack

import numpy as np
import ml_dtypes

import bass_rust
import concourse.bass as bass
import concourse.mybir as mybir
from concourse import library_config
import concourse.bass_utils as _bass_utils
from concourse.bass_utils import run_bass_kernel_spmd

if not getattr(_bass_utils, "_ldw_opt_patched", False):
    _orig_run_command = _bass_utils.run_command

    def _run_command_ldw(cmd, *a, **k):
        cmd = [("--enable-ldw-opt=true" if c == "--enable-ldw-opt=false" else c)
               for c in cmd]
        return _orig_run_command(cmd, *a, **k)

    _bass_utils.run_command = _run_command_ldw
    _bass_utils._ldw_opt_patched = True

F32 = mybir.dt.float32
BF16 = mybir.dt.bfloat16
I16 = mybir.dt.int16
U16 = mybir.dt.uint16
AF = mybir.ActivationFunctionType
ALU = mybir.AluOpType
AX = mybir.AxisListType
RED = bass_rust.ReduceOp

C_RNE = 12582912.0
NCORES = 8
B, L, DM = 16, 2048, 1024
H, D = 16, 64
BPC = B // NCORES
FC = DM // 128
TT = 512
TPS = L // TT
NTILES = BPC * TPS
TC = 128
NTC = TT // TC
HP = 8
DI = 3072
MI = DI // 128
QMAXA = 32767.0
EPS = 1e-5
LN8 = float(np.log(0.125))

BF = ml_dtypes.bfloat16


ENGINES = ("tensor", "vector", "scalar", "gpsimd", "sync")
CSEM = {"tensor": "st", "vector": "sv", "scalar": "ss", "gpsimd": "sg"}


def _merge(deps):
    m = {}
    for d in deps:
        if d is None:
            continue
        s, v = d
        if m.get(s, -1) < v:
            m[s] = v
    return m


class Buf:

    __slots__ = ("wt", "rt", "name")

    def __init__(self, name=""):
        self.wt = {}
        self.rt = {}
        self.name = name

    def deps_read(self):
        return list(self.wt.items())

    def deps_write(self):
        return list(self.wt.items()) + list(self.rt.items())


class Em:

    def __init__(self):
        self.steps = {e: [] for e in ENGINES}
        self.counts = {s: 0 for s in CSEM.values()}
        self.counts["s_cc"] = 0

    def sem(self, name):
        if name not in self.counts:
            self.counts[name] = 0
        return name

    LABEL = [None]

    def _push(self, eng, fn, deps, sem, inc):
        m = _merge(deps)
        lbl = Em.LABEL[0]
        if sem is not None:
            self.counts[sem] += inc
            tok = (sem, self.counts[sem])
        else:
            tok = None
        self.steps[eng].append((m, fn, sem, inc, lbl))
        return tok

    def do(self, eng, fn, reads=(), writes=(), rw=(), deps=(), dma=None, cc=False):
        d = list(deps)
        for b in reads:
            d += b.deps_read()
        for b in writes:
            d += b.deps_write()
        for b in rw:
            d += b.deps_write()
        if cc:
            sem, inc = "s_cc", 1
        elif dma is not None:
            sem, inc = self.sem(dma), 16
        else:
            sem, inc = CSEM[eng], 1
        tok = self._push(eng, fn, d, sem, inc)
        s, v = tok
        for b in reads:
            if b.rt.get(s, -1) < v:
                b.rt[s] = v
        for b in writes:
            b.wt = {s: v}
            b.rt = {}
        for b in rw:
            if b.wt.get(s, -1) < v:
                b.wt[s] = v
        return tok

    def snapshot(self):
        return {k: v for k, v in self.counts.items() if v > 0}

    def freeze_wave(self, bufs, semname):
        tot = self.counts[semname]
        for b in bufs:
            b.wt = {semname: tot}
        return (semname, tot)

    def replay(self, block, sems):
        for eng in ENGINES:
            steps = self.steps[eng]
            if not steps:
                continue
            sec = getattr(block, eng)

            own = CSEM.get(eng)

            def _body(e, _steps=steps, _own=own):
                credit = {}
                for deps, fn, sem, inc, lbl in _steps:
                    for s, v in deps.items():
                        if s == _own and eng == "tensor":
                            continue
                        if credit.get(s, 0) < v:
                            e.wait_ge(sems[s], v)
                            credit[s] = v
                    if fn is not None:
                        inst = fn(e)
                        if sem is not None:
                            inst.then_inc(sems[sem], inc)

            sec(_body)


def bslice(t, bi, n):
    return t[:, bi * n:(bi + 1) * n]


def build(n_cores=NCORES, taps=(), upto="all"):
    nc = bass.Bass("TRN2", target_bir_lowering=False, debug=False)
    em = Em()
    taps = set(taps)
    tap_bufs = {}

    x_h = nc.dram_tensor("x", [FC, 128, BPC * L], F32, kind="ExternalInput").ap()
    wq_h = nc.dram_tensor("wq", [DM, DM], F32, kind="ExternalInput").ap()
    wk_h = nc.dram_tensor("wk", [DM, DM], F32, kind="ExternalInput").ap()
    wv_h = nc.dram_tensor("wv", [DM, DM], F32, kind="ExternalInput").ap()
    wgp_h = nc.dram_tensor("wgp", [DM, DM], F32, kind="ExternalInput").ap()
    wo_h = nc.dram_tensor("wo", [DM, DM], F32, kind="ExternalInput").ap()
    wg1_h = nc.dram_tensor("wg1", [DM, 16], F32, kind="ExternalInput").ap()
    wg2a_h = nc.dram_tensor("wg2a", [17, DM], F32, kind="ExternalInput").ap()
    gnw_h = nc.dram_tensor("gnw", [128, 1], F32, kind="ExternalInput").ap()
    wup_h = nc.dram_tensor("wup", [DI, DM], F32, kind="ExternalInput").ap()
    wgate_h = nc.dram_tensor("wgate", [DI, DM], F32, kind="ExternalInput").ap()
    wdown_h = nc.dram_tensor("wdown", [DM, DI], F32, kind="ExternalInput").ap()
    ct_h = nc.dram_tensor("ct", [FC, 128, BPC], F32, kind="ExternalInput").ap()
    wmod_h = nc.dram_tensor("wmod", [DM, 6 * DM], F32, kind="ExternalInput").ap()
    tri_h = nc.dram_tensor("tri", [128, 3 * 128], BF16, kind="ExternalInput").ap()
    ones_h = nc.dram_tensor("onesc", [128, 2 * 128], BF16, kind="ExternalInput").ap()
    kv_h = nc.dram_tensor("kvec", [128, 4], F32, kind="ExternalInput").ap()
    mask8_h = nc.dram_tensor("mask8", [128, 1024], BF16, kind="ExternalInput").ap()
    onerow_h = nc.dram_tensor("onerow", [1, TT], BF16, kind="ExternalInput").ap()
    bone_h = nc.dram_tensor("bone", [1, 128], F32, kind="ExternalInput").ap()
    idn_h = nc.dram_tensor("idn", [128, 128], F32, kind="ExternalInput").ap()
    out_h = nc.dram_tensor("out", [FC, 128, BPC * L], F32, kind="ExternalOutput").ap()

    xmid_h = nc.dram_tensor("xmid", [FC, 128, BPC * L], F32).ap()
    h2_h = nc.dram_tensor("h2d", [FC, 128, BPC * L], BF16).ap()
    g_h = nc.dram_tensor("gspill", [NTILES, MI, 128, TT], BF16).ap()
    u_h = nc.dram_tensor("uspill", [NTILES, MI, 128, TT], BF16).ap()
    t_h = nc.dram_tensor("tspill", [NTILES, MI, 128, TT], BF16).ap()
    wupq_h = nc.dram_tensor("wupq", [DI, DM], BF16).ap()
    wgateq_h = nc.dram_tensor("wgateq", [DI, DM], BF16).ap()
    wdownq_h = nc.dram_tensor("wdownq", [DM, DI], BF16).ap()
    cc_in_h = nc.dram_tensor("cc_in", [1, 4], F32).ap()
    cc_out_h = nc.dram_tensor("cc_out", [1, 4], F32).ap()

    st = ExitStack()
    ec = st.enter_context

    def sb(name, shape, dt, scope=None, init=None):
        t = (scope or st).enter_context(nc.sbuf_tensor(name, shape, dt))
        b = Buf(name)
        if init:
            b.wt = dict(init)
        return t, b

    def ps(name, shape, scope=None, init=None):
        t = (scope or st).enter_context(nc.psum_tensor(name, shape, F32))
        b = Buf(name)
        if init:
            b.wt = dict(init)
        return t, b

    kv, b_kv = sb("kv", [128, 4], F32)
    ZERO, EPSC, LN8C, ONEC = (kv[:, i:i + 1] for i in range(4))
    gnw, b_gnw = sb("gnwt", [128, 1], F32)
    tri, b_tri = sb("trit", [128, 3 * 128], BF16)
    triA, triR, MASK = (bslice(tri, i, 128) for i in range(3))
    onesc, b_ones = sb("onest", [128, 2 * 128], BF16)
    ONES_LN, BD_RMS = bslice(onesc, 0, 128), bslice(onesc, 1, 128)
    modr, b_modr = sb("modr", [128, 96], F32)
    modg, b_modg = sb("modg", [128, 32], F32)
    modc, b_modc = sb("modc", [128, 96], F32)
    silc, b_silc = sb("silc", [128, FC * BPC], F32)
    ctb, b_ct = sb("ctb", [128, FC * BPC], F32)
    sscr, b_sscr = sb("sscr", [128, 16], F32)
    scl, b_scl = sb("scl", [128, 12], F32)
    ccs, b_ccs = sb("ccs", [1, 4], F32)
    ccr, b_ccr = sb("ccr", [1, 4], F32)
    tiny, b_tiny = sb("tiny", [128, 8], F32)
    dq_in = [sb(f"dq_in{i}", [128, 512], F32) for i in range(2)]
    dq_s, b_dqs = sb("dq_s", [128, 16], F32)
    dq_r, b_dqr = sb("dq_r", [128, 16], F32)
    dq_wr, b_dqwr = sb("dq_wr", [128, 512], F32)
    dq_o = [sb(f"dq_o{i}", [128, 512], BF16) for i in range(2)]

    idn, b_idn = sb("idn_s", [128, 128], F32)
    scrow, b_scrow = sb("scrow", [1, 16], F32)
    bone, b_bone = sb("bone_s", [1, 128], F32)

    def modcol(t, ch, bb):
        return t[:, (ch * BPC + bb):(ch * BPC + bb) + 1]

    def ga_ap(fc_, bb):
        return modg[:, (fc_ * BPC + bb):(fc_ * BPC + bb) + 1]

    def gb_ap(fc_, bb):
        return modg[:, 16 + (fc_ * BPC + bb):16 + (fc_ * BPC + bb) + 1]

    def sa_ap(fc_, bb):
        return modcol(modr, 0 * 8 + fc_, bb)

    def sb_ap(fc_, bb):
        return modcol(modr, 3 * 8 + fc_, bb)

    def ca1_ap(fc_, bb):
        return modcol(modc, 1 * 8 + fc_, bb)

    def cb1_ap(fc_, bb):
        return modcol(modc, 4 * 8 + fc_, bb)

    em.do("sync", lambda e: e.dma_start(out=kv[:], in_=kv_h), writes=(b_kv,), dma="d_c")
    em.do("sync", lambda e: e.dma_start(out=gnw[:], in_=gnw_h), writes=(b_gnw,), dma="d_c")
    em.do("sync", lambda e: e.dma_start(out=tri[:], in_=tri_h), writes=(b_tri,), dma="d_c")
    em.do("sync", lambda e: e.dma_start(out=onesc[:], in_=ones_h), writes=(b_ones,), dma="d_c")
    em.do("sync", lambda e: e.dma_start(out=ctb[:].rearrange("p (a b) -> p a b", a=FC),
          in_=ct_h.rearrange("a p b -> p a b")), writes=(b_ct,), dma="d_c")
    em.do("sync", lambda e: e.dma_start(out=bone[:], in_=bone_h), writes=(b_bone,), dma="d_c")
    em.do("sync", lambda e: e.dma_start(out=idn[:], in_=idn_h), writes=(b_idn,), dma="d_c")
    em.freeze_wave((b_kv, b_gnw, b_tri, b_ones, b_ct, b_bone, b_idn), "d_c")

    scopeA = ExitStack()
    e1, b_e1 = sb("m_e1", [128, FC * BPC], F32, scope=scopeA)
    em.do("scalar", lambda e: e.activation(e1[:], ctb[:], AF.Exp, bias=ZERO, scale=-1.0),
          reads=(b_ct, b_kv), writes=(b_e1,))
    em.do("vector", lambda e: e.tensor_scalar(e1[:], e1[:], 1.0, None, op0=ALU.add),
          rw=(b_e1,))
    em.do("vector", lambda e: e.reciprocal(e1[:], e1[:]), rw=(b_e1,))
    em.do("vector", lambda e: e.tensor_tensor(silc[:], ctb[:], e1[:], op=ALU.mult),
          reads=(b_ct, b_e1), writes=(b_silc,))

    wmb = []
    for i in range(2):
        t, b = sb(f"wmb{i}", [128, 1536], F32, scope=scopeA)
        wmb.append((t, b))
    macc, b_macc = sb("modacc", [128, 24], F32, scope=scopeA)
    mps, b_mps = ps("mod_ps", [128, 64], scope=scopeA)
    for cg in range(4):
        for kc in range(FC):
            slot = (cg * FC + kc) % 2
            t, bwm = wmb[slot]
            em.do("sync", lambda e, t=t, kc=kc, cg=cg: e.dma_start(
                out=t[:], in_=wmod_h[kc * 128:(kc + 1) * 128, cg * 1536:(cg + 1) * 1536]),
                writes=(bwm,), dma=f"d_wm{slot}")
            for mc in range(12):
                em.do("tensor", lambda e, t=t, kc=kc, mc=mc: e.matmul(
                    mps[:, mc * BPC:(mc + 1) * BPC], t[:, mc * 128:(mc + 1) * 128],
                    silc[:, kc * BPC:(kc + 1) * BPC],
                    start=True, stop=True),
                    reads=(bwm, b_silc),
                    writes=(b_mps,) if mc == 0 else (), rw=(b_mps,) if mc != 0 else ())
            if kc == 0:
                em.do("vector", lambda e: e.tensor_scalar(
                    macc[:], mps[:, 0:24], 0.0, None, op0=ALU.add),
                    reads=(b_mps,), writes=(b_macc,))
            else:
                em.do("vector", lambda e: e.tensor_tensor(
                    macc[:], macc[:], mps[:, 0:24], op=ALU.add),
                    reads=(b_mps, b_macc), writes=(b_macc,))
        em.do("scalar", lambda e, cg=cg: e.copy(modr[:, cg * 24:(cg + 1) * 24], macc[:]),
              reads=(b_macc,), rw=(b_modr,))
    for gi, j in enumerate((2, 5)):
        src = modr[:, j * 8 * BPC:(j + 1) * 8 * BPC]
        dst = modg[:, gi * 16:(gi + 1) * 16]
        em.do("vector", lambda e, s=src: e.tensor_scalar(sscr[:, 0:16], s, 0.0, None, op0=ALU.is_ge),
              reads=(b_modr,), writes=(b_sscr,))
        em.do("vector", lambda e: e.tensor_scalar(sscr[:, 0:16], sscr[:, 0:16], 2.0, -1.0, op0=ALU.mult, op1=ALU.add),
              rw=(b_sscr,))
        em.do("scalar", lambda e, s=src, d=dst: e.activation(d, s, AF.Abs, bias=ZERO, scale=1.0),
              reads=(b_modr, b_kv), rw=(b_modg,))
        em.do("vector", lambda e, d=dst: e.tensor_scalar(d, d, 0.25, None, op0=ALU.max),
              rw=(b_modg,))
        em.do("vector", lambda e, d=dst: e.tensor_tensor(d, d, sscr[:, 0:16], op=ALU.mult),
              reads=(b_sscr,), rw=(b_modg,))
    for j in (1, 4):
        em.do("vector", lambda e, j=j: e.tensor_scalar(
            modc[:, j * 16:(j + 1) * 16], modr[:, j * 16:(j + 1) * 16], 1.0, None, op0=ALU.add),
            reads=(b_modr,), rw=(b_modc,))

    scopeA.close()
    snapA = em.snapshot()

    if "mod" in taps:
        tp = nc.dram_tensor("tap_mod", [128, 96], F32, kind="ExternalOutput").ap()
        tg = nc.dram_tensor("tap_modg", [128, 32], F32, kind="ExternalOutput").ap()
        tk1 = em.do("sync", lambda e: e.dma_start(out=tp, in_=modr[:]), reads=(b_modr,), dma="d_tap")
        tk2 = em.do("sync", lambda e: e.dma_start(out=tg, in_=modg[:]), reads=(b_modg,), dma="d_tap")
        em._push("sync", None, [tk1, tk2], None, 0)


    if upto == "mod":
        block = ec(nc.Block())
        sems = {s: ec(nc.semaphore(s)) for s in em.counts}
        em.replay(block, sems)
        st.close()
        return nc

    scopeB = ExitStack()

    def sba(name, shape, dt):
        return sb(name, shape, dt, scope=scopeB, init=snapA)

    mask8, b_mask8 = sba("mask8t", [128, 1024], BF16)
    em.do("sync", lambda e: e.dma_start(out=mask8[:], in_=mask8_h), writes=(b_mask8,), dma="d_c2")
    em.freeze_wave((b_mask8,), "d_c2")

    wqs, b_wq = sba("wqs", [128, FC * DM], BF16)
    wks, b_wk = sba("wks", [128, FC * DM], BF16)
    wvs, b_wv = sba("wvs", [128, FC * DM], BF16)
    wgs, b_wg = sba("wgs", [128, FC * DM], BF16)
    wos, b_wo = sba("wos", [128, FC * DM], BF16)
    wg1s, b_wg1 = sba("wg1s", [128, FC * 16], BF16)
    wg2s, b_wg2 = sba("wg2s", [32, DM], BF16)
    for wsb, bw, wh in ((wqs, b_wq, wq_h), (wks, b_wk, wk_h), (wvs, b_wv, wv_h),
                        (wgs, b_wg, wgp_h), (wos, b_wo, wo_h)):
        em.do("gpsimd", lambda e, o=wsb, i=wh: e.dma_start(
            out=o[:].rearrange("p (a m) -> p a m", a=FC),
            in_=i.rearrange("(a p) m -> p a m", p=128)), writes=(bw,), dma="d_wg")
    em.do("gpsimd", lambda e: e.dma_start(
        out=wg1s[:].rearrange("p (a m) -> p a m", a=FC),
        in_=wg1_h.rearrange("(a p) m -> p a m", p=128)), writes=(b_wg1,), dma="d_wg")
    em.do("gpsimd", lambda e: e.dma_start(out=wg2s[0:17, :], in_=wg2a_h), writes=(b_wg2,), dma="d_wg")
    em.freeze_wave((b_wq, b_wk, b_wv, b_wg, b_wo, b_wg1, b_wg2), "d_wg")

    xb, b_xb = sba("xb", [128, FC * TT], BF16)
    x2s = [sba(f"x2s{i}", [128, TT], BF16) for i in range(2)]
    mu_bf, b_mu = sba("mu_bf", [128, TT], BF16)
    mu2, b_mu2 = sba("mu2", [128, TT], BF16)
    varr, b_varr = sba("varr", [128, TT], BF16)
    rstd_bf, b_rstd = sba("rstd_bf", [128, TT], BF16)
    xc1W, b_xc1W = sba("xc1W", [128, FC * TT], BF16)
    hsb, b_h = sba("hsb", [128, FC * TT], BF16)
    qsb, b_q = sba("qsb", [128, FC * TT], BF16)
    ksb, b_k = sba("ksb", [128, FC * TT], BF16)
    gsig, b_gsig = sba("gsig", [128, FC * TT], BF16)
    ktm, b_ktm = sba("ktm", [128, NTC * DM], BF16)
    vtm, b_vtm = sba("vtm", [128, NTC * DM], BF16)
    gks, b_gk = sba("gks", [128, NTC * DM], BF16)
    z1s, b_z1 = sba("z1s", [128, TT], BF16)
    eaq = [sba(f"eaq{i}", [128, TT], BF16) for i in range(2)]
    ean = [sba(f"ean{i}", [128, TT], BF16) for i in range(2)]
    eaend = [sba(f"eaend{i}", [128, NTC], F32) for i in range(2)]
    eras = [sba(f"era{i}", [128, TT], BF16) for i in range(2)]
    qts = [sba(f"qt{i}", [128, TT], BF16) for i in range(2)]
    kts = [sba(f"kt{i}", [128, TT], BF16) for i in range(2)]
    kbs = [sba(f"kb{i}", [128, TT], BF16) for i in range(2)]
    pms = [sba("pm0", [128, 1024], BF16)] * 2
    o2s = [sba("o20", [128, TT], BF16)] * 2
    rmsscr, b_rmsscr = sba("rmsscr", [128, TT], F32)
    rstdr = [sba("rstdr0", [128, TT], BF16)] * 2
    rgs = [sba("rg0", [128, TT], BF16)] * 2
    atis, b_ati = ksb, b_k
    S4 = [[sba(f"S{bb}_{i}", [128, HP * 64], BF16) for i in range(2)] for bb in range(BPC)]
    xres = [sba(f"xres{i}", [128, TT], F32) for i in range(2)]
    xmsub = [sba(f"xms{i}", [128, TT], F32) for i in range(2)]
    xb2, b_xb2 = xb, b_xb
    m1c, b_m1c = sba("m1c", [128, NTILES * FC], F32)

    dq_by_w = {}
    for tag, wh, wqh, rows, cols in (
            ("g", wgate_h, wgateq_h, DI, DM), ("u", wup_h, wupq_h, DI, DM),
            ("d", wdown_h, wdownq_h, DM, DI)):
        dq_by_w[tag] = [(wh, wqh, rc, pc, tag) for rc in range(rows // 128)
                        for pc in range(cols // 512)]
    dq_steps = dq_by_w["g"]

    def emit_dq(step, slot):
        wh, wqh, rc, pc, tag = step
        ti, bi = dq_in[slot % 2]
        to, bo = dq_o[slot % 2]
        em.do("sync", lambda e, t=ti, wh=wh, rc=rc, pc=pc: e.dma_start(
            out=t[:], in_=wh[rc * 128:(rc + 1) * 128, pc * 512:(pc + 1) * 512]),
            writes=(bi,), dma=f"d_dq{slot % 2}")
        em.do("vector", lambda e, t=ti: e.tensor_reduce(
            dq_s[:], t[:].rearrange("p (a b) -> p a b", b=32),
            axis=AX.X, op=ALU.max, apply_absolute_value=True),
            reads=(bi,), writes=(b_dqs,))
        em.do("vector", lambda e: e.tensor_scalar(
            dq_s[:], dq_s[:], 1.0 / 7.0, 1e-12, op0=ALU.mult, op1=ALU.max), rw=(b_dqs,))
        em.do("vector", lambda e: e.reciprocal(dq_r[:], dq_s[:]), reads=(b_dqs,), writes=(b_dqr,))
        em.do("vector", lambda e, t=ti: e.tensor_tensor(
            dq_wr[:].rearrange("p (a b) -> p a b", b=32),
            t[:].rearrange("p (a b) -> p a b", b=32),
            dq_r[:].rearrange("p (a b) -> p a b", b=1).broadcast_to([128, 16, 32]), op=ALU.mult),
            reads=(bi, b_dqr), writes=(b_dqwr,))
        em.do("vector", lambda e: e.tensor_scalar(
            dq_wr[:], dq_wr[:], C_RNE, -C_RNE, op0=ALU.add, op1=ALU.add), rw=(b_dqwr,))
        em.do("vector", lambda e, t=to: e.tensor_tensor(
            t[:].rearrange("p (a b) -> p a b", b=32),
            dq_wr[:].rearrange("p (a b) -> p a b", b=32),
            dq_s[:].rearrange("p (a b) -> p a b", b=1).broadcast_to([128, 16, 32]), op=ALU.mult),
            reads=(b_dqwr, b_dqs), writes=(bo,))
        em.do("scalar", lambda e, t=to, wqh=wqh, rc=rc, pc=pc: e.dma_start(
            out=wqh[rc * 128:(rc + 1) * 128, pc * 512:(pc + 1) * 512], in_=t[:]),
            reads=(bo,), dma=f"d_dqo_{tag}{slot % 2}")

    pp = [ps(f"pp{i}", [128, TT], scope=scopeB, init=snapA) for i in range(2)]
    pa, b_pa = ps("pa", [128, TT], scope=scopeB, init=snapA)
    pr, b_pr = pa, b_pa
    pP, b_pP = ps("pP", [128, 1024], scope=scopeB, init=snapA)
    pOs = [ps(f"pO{i}", [128, TT], scope=scopeB, init=snapA) for i in range(2)]
    pkr, b_pkr = ps("pkr", [128, TT], scope=scopeB, init=snapA)

    em.do("sync", lambda e: e.dma_start(out=z1s[16:17, :], in_=onerow_h), rw=(b_z1,), dma="d_c3")
    em.freeze_wave((b_z1,), "d_c3")

    def ln_pass(xbt, b_xbt, dst, b_dst, cmod_ap, smod_ap, bb, track=None):
        p0t, b_p0 = pp[0]
        p1t, b_p1 = pp[1]
        for kc in range(FC):
            em.do("tensor", lambda e, kc=kc: e.matmul(
                p0t[:], ONES_LN, bslice(xbt, kc, TT), start=(kc == 0), stop=(kc == FC - 1)),
                reads=(b_xbt, b_ones), writes=(b_p0,) if kc == 0 else (), rw=(b_p0,) if kc else ())
        for kc in range(FC):
            t2, bx2 = x2s[kc % 2]
            em.do("scalar", lambda e, kc=kc, t2=t2: e.activation(
                t2[:], bslice(xbt, kc, TT), AF.Square, bias=ZERO, scale=1.0),
                reads=(b_xbt, b_kv), writes=(bx2,))
            em.do("tensor", lambda e, kc=kc, t2=t2: e.matmul(
                p1t[:], ONES_LN, t2[:], start=(kc == 0), stop=(kc == FC - 1)),
                reads=(bx2, b_ones), writes=(b_p1,) if kc == 0 else (), rw=(b_p1,) if kc else ())
        em.do("scalar", lambda e: e.copy(mu_bf[:], p0t[:]), reads=(b_p0,), writes=(b_mu,))
        em.do("scalar", lambda e: e.activation(mu2[:], mu_bf[:], AF.Square, bias=ZERO, scale=1.0),
              reads=(b_mu, b_kv), writes=(b_mu2,))
        em.do("vector", lambda e: e.tensor_tensor(varr[:], p1t[:], mu2[:], op=ALU.subtract),
              reads=(b_p1, b_mu2), writes=(b_varr,))
        em.do("scalar", lambda e: e.activation(rmsscr[:], varr[:], AF.Ln, bias=EPSC, scale=1.0),
              reads=(b_varr, b_kv), writes=(b_rmsscr,))
        em.do("scalar", lambda e: e.activation(rstd_bf[:], rmsscr[:], AF.Exp, bias=ZERO, scale=-0.5),
              reads=(b_rmsscr, b_kv), writes=(b_rstd,))
        em.do("vector", lambda e: e.tensor_tensor(
            xc1W[:].rearrange("p (a b) -> p a b", b=TT),
            xbt[:].rearrange("p (a b) -> p a b", b=TT),
            mu_bf[:].rearrange("p (o b) -> p o b", o=1).broadcast_to([128, FC, TT]),
            op=ALU.subtract), reads=(b_xbt, b_mu), writes=(b_xc1W,))
        em.do("vector", lambda e: e.tensor_tensor(
            xc1W[:].rearrange("p (a b) -> p a b", b=TT),
            xc1W[:].rearrange("p (a b) -> p a b", b=TT),
            rstd_bf[:].rearrange("p (o b) -> p o b", o=1).broadcast_to([128, FC, TT]),
            op=ALU.mult), reads=(b_rstd,), rw=(b_xc1W,))
        for fc_ in range(FC):
            em.do("vector", lambda e, fc_=fc_: e.tensor_scalar(
                bslice(dst, fc_, TT), bslice(xc1W, fc_, TT), cmod_ap(fc_, bb),
                smod_ap(fc_, bb), op0=ALU.mult, op1=ALU.add),
                reads=(b_xc1W, b_modc, b_modr), rw=(b_dst,))


    def proj_full(wsb, b_w, dst, b_dst, evac):
        for mc in range(FC):
            pt, b_pt = pp[mc % 2]
            for kc in range(FC):
                em.do("tensor", lambda e, pt=pt, mc=mc, kc=kc: e.matmul(
                    pt[:], wsb[:, kc * DM + mc * 128:kc * DM + (mc + 1) * 128],
                    bslice(hsb, kc, TT), start=(kc == 0), stop=(kc == FC - 1)),
                    reads=(b_w, b_h), writes=(b_pt,) if kc == 0 else (), rw=(b_pt,) if kc else ())
            evac(mc, pt, b_pt)

    def proj_tm(wsb, b_w, dst, b_dst):
        for tc in range(NTC):
            for half in range(2):
                pt, b_pt = pp[(tc * 2 + half) % 2]
                for kc in range(FC):
                    em.do("tensor", lambda e, pt=pt, tc=tc, half=half, kc=kc: e.matmul(
                        pt[:], hsb[:, kc * TT + tc * 128:kc * TT + (tc + 1) * 128],
                        wsb[:, kc * DM + half * 512:kc * DM + (half + 1) * 512],
                        start=(kc == 0), stop=(kc == FC - 1)),
                        reads=(b_w, b_h), writes=(b_pt,) if kc == 0 else (), rw=(b_pt,) if kc else ())
                em.do("scalar", lambda e, pt=pt, tc=tc, half=half: e.copy(
                    dst[:, tc * DM + half * 512:tc * DM + (half + 1) * 512], pt[:]),
                    reads=(b_pt,), rw=(b_dst,))

    tiles = [(bb, tt) for tt in range(TPS) for bb in range(BPC)]
    dq_per_tile = (len(dq_steps) + NTILES - 1) // NTILES

    def ln1_block(it2):
        bb2, tt2 = tiles[it2]
        tok2 = bb2 * L + tt2 * TT
        em.do("gpsimd", lambda e, tok2=tok2: e.dma_start(
            out=xb[:].rearrange("p (a t) -> p a t", a=FC),
            in_=x_h[:, :, tok2:tok2 + TT].rearrange("a p t -> p a t")),
            writes=(b_xb,), dma="d_xb")
        ln_pass(xb, b_xb, hsb, b_h, ca1_ap, sa_ap, bb2)

    ln1_block(0)
    for it, (bb, tt) in enumerate(tiles):
        tok0 = bb * L + tt * TT
        first_chunk_of_seq = (tt == 0)
        dq_lo = it * dq_per_tile
        dq_hi = min((it + 1) * dq_per_tile, len(dq_steps))
        dq_cursor = [dq_lo]

        def dq_drip(n=3):
            while dq_cursor[0] < dq_hi and n > 0:
                emit_dq(dq_steps[dq_cursor[0]], dq_cursor[0])
                dq_cursor[0] += 1
                n -= 1
        def ev_q(mc, pt, b_pt):
            em.do("vector", lambda e, mc=mc, pt=pt: e.tensor_scalar(
                bslice(qsb, mc, TT), pt[:], 0.0, None, op0=ALU.add),
                reads=(b_pt,), rw=(b_q,))
        def ev_k(mc, pt, b_pt):
            em.do("scalar", lambda e, mc=mc, pt=pt: e.copy(bslice(ksb, mc, TT), pt[:]),
                  reads=(b_pt,), rw=(b_k,))
        def ev_g(mc, pt, b_pt):
            em.do("scalar", lambda e, mc=mc, pt=pt: e.activation(
                bslice(gsig, mc, TT), pt[:], AF.Silu, bias=ZERO, scale=1.0),
                reads=(b_pt, b_kv), rw=(b_gsig,))
        proj_full(wqs, b_wq, qsb, b_q, ev_q)
        dq_drip(1)
        proj_full(wks, b_wk, ksb, b_k, ev_k)
        dq_drip(1)
        proj_tm(wks, b_wk, ktm, b_ktm)
        dq_drip(1)
        pzt, b_pzt = pp[0]
        for kc in range(FC):
            em.do("tensor", lambda e, kc=kc: e.matmul(
                pzt[0:16, :], wg1s[:, kc * 16:(kc + 1) * 16], bslice(hsb, kc, TT),
                start=(kc == 0), stop=(kc == FC - 1)),
                reads=(b_wg1, b_h), writes=(b_pzt,) if kc == 0 else (), rw=(b_pzt,) if kc else ())
        em.do("scalar", lambda e: e.copy(z1s[0:16, :], pzt[0:16, :]), reads=(b_pzt,), rw=(b_z1,))
        tpm0, bpm0 = pms[0]
        for tc in range(NTC):
            for half in range(2):
                em.do("tensor", lambda e, tc=tc, half=half: e.matmul(
                    pP[:, half * 512:(half + 1) * 512], z1s[0:17, tc * 128:(tc + 1) * 128],
                    wg2s[0:17, half * 512:(half + 1) * 512], start=True, stop=True),
                    reads=(b_z1, b_wg2), writes=(b_pP,) if half == 0 else (), rw=(b_pP,) if half else ())
            em.do("scalar", lambda e, tpm0=tpm0: e.activation(
                tpm0[:], pP[:], AF.Exp, bias=ZERO, scale=-1.0),
                reads=(b_pP, b_kv), writes=(bpm0,))
            em.do("scalar", lambda e, tc=tc, tpm0=tpm0: e.activation(
                gks[:, tc * DM:tc * DM + 1024], tpm0[:], AF.Ln, bias=ONEC, scale=1.0),
                reads=(bpm0, b_kv), rw=(b_gk,))
        proj_tm(wvs, b_wv, vtm, b_vtm)
        dq_drip(1)
        proj_full(wgs, b_wg, gsig, b_gsig, ev_g)
        dq_drip(2)
        if it + 1 < NTILES:
            ln1_block(it + 1)
        def scan_stage1(hp):
            sl = hp % 2
            te, be = eaq[sl]
            tn, bn = ean[sl]
            tee, bee = eaend[sl]
            tr, br = eras[sl]
            tqt, bqt = qts[sl]
            tkt, bkt = kts[sl]
            tkb, bkb = kbs[sl]
            for tc in range(NTC):
                em.do("tensor", lambda e, tc=tc, hp=hp: e.matmul(
                    pa[:, tc * 128:(tc + 1) * 128],
                    gks[:, tc * DM + hp * 128:tc * DM + (hp + 1) * 128], triA,
                    start=True, stop=True),
                    reads=(b_gk, b_tri), writes=(b_pa,) if tc == 0 else (), rw=(b_pa,) if tc else ())
            em.do("scalar", lambda e, te=te: e.activation(te[:], pa[:], AF.Exp, bias=LN8C, scale=1.0),
                  reads=(b_pa, b_kv), writes=(be,))
            em.do("scalar", lambda e, tn=tn: e.activation(tn[:], pa[:], AF.Exp, bias=ZERO, scale=-1.0),
                  reads=(b_pa, b_kv), writes=(bn,))
            em.do("scalar", lambda e, tee=tee: e.activation(
                tee[:], pa[:, 127:TT:128], AF.Exp, bias=ZERO, scale=1.0),
                reads=(b_pa, b_kv), writes=(bee,))
            for tc in range(NTC):
                em.do("tensor", lambda e, tc=tc, hp=hp: e.matmul(
                    pr[:, tc * 128:(tc + 1) * 128], triR,
                    gks[:, tc * DM + hp * 128:tc * DM + (hp + 1) * 128],
                    start=True, stop=True),
                    reads=(b_gk, b_tri), writes=(b_pr,) if tc == 0 else (), rw=(b_pr,) if tc else ())
            em.do("scalar", lambda e, tr=tr: e.activation(tr[:], pr[:], AF.Exp, bias=ZERO, scale=1.0),
                  reads=(b_pr, b_kv), writes=(br,))
            em.do("vector", lambda e, tqt=tqt, te=te, hp=hp: e.tensor_tensor(
                tqt[:], bslice(qsb, hp, TT), te[:], op=ALU.mult),
                reads=(b_q, be), writes=(bqt,))
            em.do("vector", lambda e, tkt=tkt, tn=tn, hp=hp: e.tensor_tensor(
                tkt[:], bslice(ksb, hp, TT), tn[:], op=ALU.mult),
                reads=(b_k, bn), writes=(bkt,))
            for tc in range(NTC):
                em.do("vector", lambda e, tkb=tkb, tr=tr, tc=tc, hp=hp: e.tensor_tensor(
                    tkb[:, tc * 128:(tc + 1) * 128],
                    ktm[:, tc * DM + hp * 128:tc * DM + (hp + 1) * 128],
                    tr[:, tc * 128:(tc + 1) * 128], op=ALU.mult),
                    reads=(b_ktm, br), writes=(bkb,) if tc == 0 else (), rw=(bkb,) if tc else ())

        def scan_stage2(hp):
            sl = hp % 2
            pO, b_pO = pOs[sl]
            te, be = eaq[sl]
            tee, bee = eaend[sl]
            tqt, bqt = qts[sl]
            tkt, bkt = kts[sl]
            tkb, bkb = kbs[sl]
            tpm, bpm = pms[0]
            to2, bo2 = o2s[0]
            trs, brs = rstdr[0]
            trg, brg = rgs[0]
            for tc in range(NTC):
                em.do("tensor", lambda e, tkt=tkt, tqt=tqt, tc=tc: e.matmul(
                    pP[:, tc * 128:(tc + 1) * 128],
                    tkt[0:64, tc * 128:(tc + 1) * 128], tqt[0:64, tc * 128:(tc + 1) * 128],
                    start=True, stop=True),
                    reads=(bkt, bqt), writes=(b_pP,) if tc == 0 else (), rw=(b_pP,) if tc else ())
                em.do("tensor", lambda e, tkt=tkt, tqt=tqt, tc=tc: e.matmul(
                    pP[:, 512 + tc * 128:512 + (tc + 1) * 128],
                    tkt[64:128, tc * 128:(tc + 1) * 128], tqt[64:128, tc * 128:(tc + 1) * 128],
                    start=True, stop=True, tile_position=(64, 0)),
                    reads=(bkt, bqt), rw=(b_pP,))
            em.do("vector", lambda e, tpm=tpm: e.tensor_tensor(
                tpm[:], pP[:], mask8[:], op=ALU.mult),
                reads=(b_pP, b_mask8), writes=(bpm,))
            for tc in range(NTC):
                cur = tc % 2
                Sold, b_Sold = S4[bb][cur]
                Snew, b_Snew = S4[bb][1 - cur]
                glob_first = first_chunk_of_seq and tc == 0
                em.do("tensor", lambda e, tc=tc, hp=hp, tpm=tpm, gf=glob_first: e.matmul(
                    pO[0:64, tc * 128:(tc + 1) * 128],
                    vtm[:, tc * DM + hp * 128:tc * DM + hp * 128 + 64],
                    tpm[:, tc * 128:(tc + 1) * 128],
                    start=True, stop=gf),
                    reads=(b_vtm, bpm), writes=(b_pO,) if tc == 0 else (), rw=(b_pO,) if tc else ())
                if not glob_first:
                    em.do("tensor", lambda e, tc=tc, hp=hp, Sold=Sold, tqt=tqt: e.matmul(
                        pO[0:64, tc * 128:(tc + 1) * 128],
                        Sold[0:64, hp * 64:(hp + 1) * 64], tqt[0:64, tc * 128:(tc + 1) * 128],
                        start=False, stop=True),
                        reads=(b_Sold, bqt), rw=(b_pO,))
                em.do("tensor", lambda e, tc=tc, hp=hp, tpm=tpm, gf=glob_first: e.matmul(
                    pO[64:128, tc * 128:(tc + 1) * 128],
                    vtm[:, tc * DM + hp * 128 + 64:tc * DM + (hp + 1) * 128],
                    tpm[:, 512 + tc * 128:512 + (tc + 1) * 128],
                    start=True, stop=gf, tile_position=(0, 64)),
                    reads=(b_vtm, bpm), rw=(b_pO,))
                if not glob_first:
                    em.do("tensor", lambda e, tc=tc, hp=hp, Sold=Sold, tqt=tqt: e.matmul(
                        pO[64:128, tc * 128:(tc + 1) * 128],
                        Sold[64:128, hp * 64:(hp + 1) * 64], tqt[64:128, tc * 128:(tc + 1) * 128],
                        start=False, stop=True, tile_position=(64, 64)),
                        reads=(b_Sold, bqt), rw=(b_pO,))
                em.do("tensor", lambda e, tc=tc, hp=hp, tkb=tkb: e.matmul(
                    pkr[0:64, tc * 64:(tc + 1) * 64],
                    tkb[:, tc * 128:tc * 128 + 64],
                    vtm[:, tc * DM + hp * 128:tc * DM + hp * 128 + 64],
                    start=True, stop=True),
                    reads=(bkb, b_vtm), writes=(b_pkr,) if tc == 0 else (), rw=(b_pkr,) if tc else ())
                em.do("tensor", lambda e, tc=tc, hp=hp, tkb=tkb: e.matmul(
                    pkr[64:128, tc * 64:(tc + 1) * 64],
                    tkb[:, tc * 128 + 64:(tc + 1) * 128],
                    vtm[:, tc * DM + hp * 128 + 64:tc * DM + (hp + 1) * 128],
                    start=True, stop=True, tile_position=(0, 64)),
                    reads=(bkb, b_vtm), rw=(b_pkr,))
                if glob_first:
                    em.do("vector", lambda e, tc=tc, hp=hp, Snew=Snew: e.tensor_scalar(
                        Snew[:, hp * 64:(hp + 1) * 64], pkr[:, tc * 64:(tc + 1) * 64],
                        0.0, None, op0=ALU.add),
                        reads=(b_pkr,), rw=(b_Snew,))
                else:
                    em.do("vector", lambda e, tc=tc, hp=hp, Snew=Snew, Sold=Sold, tee=tee: e.scalar_tensor_tensor(
                        Snew[:, hp * 64:(hp + 1) * 64], Sold[:, hp * 64:(hp + 1) * 64],
                        tee[:, tc:tc + 1], pkr[:, tc * 64:(tc + 1) * 64],
                        op0=ALU.mult, op1=ALU.add),
                        reads=(b_Sold, bee, b_pkr), rw=(b_Snew,))
                em.do("scalar", lambda e, tc=tc, to2=to2: e.activation(
                    to2[:, tc * 128:(tc + 1) * 128], pO[:, tc * 128:(tc + 1) * 128],
                    AF.Square, bias=ZERO, scale=1.0),
                    reads=(b_pO, b_kv), writes=(bo2,) if tc == 0 else (), rw=(bo2,) if tc else ())
                em.do("tensor", lambda e, tc=tc, to2=to2: e.matmul(
                    pkr[:, 256 + (tc % 2) * 128:256 + (tc % 2 + 1) * 128],
                    BD_RMS, to2[:, tc * 128:(tc + 1) * 128],
                    start=True, stop=True),
                    reads=(bo2, b_ones), rw=(b_pkr,))
                if tc % 2 == 1:
                    em.do("scalar", lambda e, tc=tc: e.activation(
                        rmsscr[:, (tc - 1) * 128:(tc + 1) * 128], pkr[:, 256:512],
                        AF.Ln, bias=EPSC, scale=1.0),
                        reads=(b_pkr, b_kv), rw=(b_rmsscr,))
            em.do("scalar", lambda e, trs=trs: e.activation(trs[:], rmsscr[:], AF.Exp, bias=ZERO, scale=-0.5),
                  reads=(b_rmsscr, b_kv), writes=(brs,))
            em.do("vector", lambda e, trg=trg, trs=trs, hp=hp: e.tensor_tensor(
                trg[:], trs[:], bslice(gsig, hp, TT), op=ALU.mult),
                reads=(brs, b_gsig), writes=(brg,))
            em.do("vector", lambda e, trg=trg, hp=hp: e.scalar_tensor_tensor(
                bslice(atis, hp, TT), trg[:], gnw[:, 0:1], pO[:],
                op0=ALU.mult, op1=ALU.mult),
                reads=(brg, b_gnw, b_pO), rw=(b_ati,))

        scan_stage1(0)
        for hp in range(HP):
            if hp + 1 < HP:
                scan_stage1(hp + 1)
            scan_stage2(hp)
        dq_drip(99)
        for mc in range(FC):
            xr, b_xr = xres[mc % 2]
            em.do("sync", lambda e, mc=mc, tok0=tok0, xr=xr: e.dma_start(
                out=xr[:], in_=x_h[mc, :, tok0:tok0 + TT]), writes=(b_xr,), dma=f"d_xr{mc % 2}")
            pt, b_pt = pp[mc % 2]
            for kc in range(FC):
                em.do("tensor", lambda e, pt=pt, mc=mc, kc=kc: e.matmul(
                    pt[:], wos[:, kc * DM + mc * 128:kc * DM + (mc + 1) * 128],
                    bslice(atis, kc, TT), start=(kc == 0), stop=(kc == FC - 1)),
                    reads=(b_wo, b_ati), writes=(b_pt,) if kc == 0 else (), rw=(b_pt,) if kc else ())
            xm, b_xm = xmsub[mc % 2]
            em.do("vector", lambda e, pt=pt, xr=xr, xm=xm, mc=mc, bb=bb: e.scalar_tensor_tensor(
                xm[:], pt[:], ga_ap(mc, bb), xr[:], op0=ALU.mult, op1=ALU.add),
                reads=(b_pt, b_modg, b_xr), writes=(b_xm,))
            em.do("sync", lambda e, xm=xm, mc=mc, tok0=tok0: e.dma_start(
                out=xmid_h[mc, :, tok0:tok0 + TT], in_=xm[:]), reads=(b_xm,), dma=f"d_xm{mc % 2}")
            em.do("scalar", lambda e, xm=xm, mc=mc: e.copy(bslice(xb2, mc, TT), xm[:]),
                  reads=(b_xm,), rw=(b_xb2,))
        p0t, b_p0 = pp[0]
        p1t, b_p1 = pp[1]
        for kc in range(FC):
            em.do("tensor", lambda e, kc=kc: e.matmul(
                p0t[:], ONES_LN, bslice(xb2, kc, TT), start=(kc == 0), stop=(kc == FC - 1)),
                reads=(b_xb2, b_ones), writes=(b_p0,) if kc == 0 else (), rw=(b_p0,) if kc else ())
        for kc in range(FC):
            t2, bx2 = x2s[kc % 2]
            em.do("scalar", lambda e, kc=kc, t2=t2: e.activation(
                t2[:], bslice(xb2, kc, TT), AF.Square, bias=ZERO, scale=1.0),
                reads=(b_xb2, b_kv), writes=(bx2,))
            em.do("tensor", lambda e, kc=kc, t2=t2: e.matmul(
                p1t[:], ONES_LN, t2[:], start=(kc == 0), stop=(kc == FC - 1)),
                reads=(bx2, b_ones), writes=(b_p1,) if kc == 0 else (), rw=(b_p1,) if kc else ())
        em.do("scalar", lambda e: e.copy(mu_bf[:], p0t[:]), reads=(b_p0,), writes=(b_mu,))
        em.do("scalar", lambda e: e.activation(mu2[:], mu_bf[:], AF.Square, bias=ZERO, scale=1.0),
              reads=(b_mu, b_kv), writes=(b_mu2,))
        em.do("vector", lambda e: e.tensor_tensor(varr[:], p1t[:], mu2[:], op=ALU.subtract),
              reads=(b_p1, b_mu2), writes=(b_varr,))
        em.do("scalar", lambda e: e.activation(rmsscr[:], varr[:], AF.Ln, bias=EPSC, scale=1.0),
              reads=(b_varr, b_kv), writes=(b_rmsscr,))
        em.do("scalar", lambda e: e.activation(rstd_bf[:], rmsscr[:], AF.Exp, bias=ZERO, scale=-0.5),
              reads=(b_rmsscr, b_kv), writes=(b_rstd,))
        em.do("vector", lambda e: e.tensor_tensor(
            xc1W[:].rearrange("p (a b) -> p a b", b=TT),
            xb2[:].rearrange("p (a b) -> p a b", b=TT),
            mu_bf[:].rearrange("p (o b) -> p o b", o=1).broadcast_to([128, FC, TT]),
            op=ALU.subtract), reads=(b_xb2, b_mu), writes=(b_xc1W,))
        em.do("vector", lambda e: e.tensor_tensor(
            xc1W[:].rearrange("p (a b) -> p a b", b=TT),
            xc1W[:].rearrange("p (a b) -> p a b", b=TT),
            rstd_bf[:].rearrange("p (o b) -> p o b", o=1).broadcast_to([128, FC, TT]),
            op=ALU.mult), reads=(b_rstd,), rw=(b_xc1W,))
        for fc_ in range(FC):
            em.do("vector", lambda e, fc_=fc_, bb=bb: e.tensor_scalar(
                bslice(xc1W, fc_, TT), bslice(xc1W, fc_, TT), cb1_ap(fc_, bb),
                sb_ap(fc_, bb), op0=ALU.mult, op1=ALU.add),
                reads=(b_modc, b_modr), rw=(b_xc1W,))
        em.do("vector", lambda e, it=it: e.tensor_reduce(
            m1c[:, it * FC:(it + 1) * FC],
            xc1W[:].rearrange("p (a b) -> p a b", b=TT), axis=AX.X, op=ALU.max,
            apply_absolute_value=True), reads=(b_xc1W,), rw=(b_m1c,))
        em.do("gpsimd", lambda e, tok0=tok0: e.dma_start(
            out=h2_h[:, :, tok0:tok0 + TT].rearrange("a p b -> p a b"),
            in_=xc1W[:].rearrange("p (a b) -> p a b", b=TT)),
            reads=(b_xc1W,), dma=f"d_h2{it % 2}")

    b_xmid_dram = Buf("xmid_d")
    b_h2_dram = Buf("h2_d")
    for i in range(2):
        b_xmid_dram.wt[f"d_xm{i}"] = em.counts.get(f"d_xm{i}", 0)
    for i in range(2):
        b_h2_dram.wt[f"d_h2{i}"] = em.counts.get(f"d_h2{i}", 0)
    b_wgq_dram = Buf("wgq_d")
    b_wgq_dram.wt = {"d_dqo_g0": em.counts.get("d_dqo_g0", 0),
                     "d_dqo_g1": em.counts.get("d_dqo_g1", 0)}

    if "attn" in taps:
        txm = nc.dram_tensor("tap_xmid", [FC, 128, BPC * L], F32, kind="ExternalOutput").ap()
        th2 = nc.dram_tensor("tap_h2", [FC, 128, BPC * L], BF16, kind="ExternalOutput").ap()
        t1 = em.do("sync", lambda e: e.dma_start(out=txm, in_=xmid_h), reads=(b_xmid_dram,), dma="d_tap")
        t2 = em.do("sync", lambda e: e.dma_start(out=th2, in_=h2_h), reads=(b_h2_dram,), dma="d_tap")
        em._push("sync", None, [t1, t2], None, 0)

    if upto == "attn":
        with nc.Block() as block:
            sems = {s: ec(nc.semaphore(s)) for s in em.counts}
            em.replay(block, sems)
        scopeB.close()
        st.close()
        return nc

    SC_R1, SC_S1, SC_R3, SC_S3, SC_R2, SC_S2, SC_K1, SC_K3, SC_S5 = range(9)

    def scol(c):
        return scl[:, c:c + 1]

    def cc_round(src_col_fns, out_fns):
        for col, fn in src_col_fns:
            fn()
        em.do("gpsimd", lambda e: e.dma_start(out=cc_in_h, in_=ccs[:]),
              reads=(b_ccs,), dma="d_cci")
        b_ccin = Buf("ccin")
        b_ccin.wt = {"d_cci": em.counts["d_cci"]}
        em.do("gpsimd", lambda e: e.collective_compute(
            "AllReduce", ALU.max, replica_groups=[list(range(n_cores))],
            ins=[cc_in_h], outs=[cc_out_h]),
            reads=(b_ccin,), writes=(b_ccout,), cc=True)
        em.do("gpsimd", lambda e: e.dma_start(out=ccr[:], in_=cc_out_h),
              reads=(b_ccout,), writes=(b_ccr,), dma="d_ccr")
        for fn in out_fns:
            fn()

    b_ccout = Buf("ccout")

    def finalize_max(mx, b_mx, ncols, col, psum_pair):
        pt, b_pt = psum_pair
        em.do("vector", lambda e: e.tensor_reduce(
            tiny[:, 0:1], mx[:, 0:ncols], axis=AX.X, op=ALU.max),
            reads=(b_mx,), writes=(b_tiny,))
        em.do("tensor", lambda e, pt=pt: e.transpose(pt[0:1, 0:128], tiny[:, 0:1], idn[:]),
              reads=(b_tiny, b_idn), writes=(b_pt,))
        em.do("vector", lambda e, col=col, pt=pt: e.tensor_reduce(
            ccs[0:1, col:col + 1], pt[0:1, 0:128], axis=AX.X, op=ALU.max),
            reads=(b_pt,), rw=(b_ccs,))

    def bcast_scales(psum_pair):
        pt, b_pt = psum_pair
        em.do("tensor", lambda e, pt=pt: e.matmul(
            pt[:, 0:12], bone[:], scrow[0:1, 0:12], start=True, stop=True),
            reads=(b_bone, b_scrow), writes=(b_pt,))
        em.do("vector", lambda e, pt=pt: e.tensor_scalar(
            scl[:], pt[:, 0:12], 1.0, None, op0=ALU.mult),
            reads=(b_pt,), writes=(b_scl,))

    def scale_pair(src_ap, b_src, rcol, scol_, psum_pair):
        def fns():
            em.do("vector", lambda e, c=scol_: e.tensor_scalar(
                scrow[0:1, c:c + 1], src_ap, 1.0 / 32767.0, 1e-12,
                op0=ALU.mult, op1=ALU.max), reads=(b_src,), rw=(b_scrow,))
            em.do("vector", lambda e, c=rcol, cs=scol_: e.reciprocal(
                scrow[0:1, c:c + 1], scrow[0:1, cs:cs + 1]), rw=(b_scrow,))
            bcast_scales(psum_pair)
        return fns

    em.do("vector", lambda e: e.memset(ccs[:], 0.0), writes=(b_ccs,))
    em.do("vector", lambda e: e.memset(scl[:], 0.0), writes=(b_scl,))
    em.do("vector", lambda e: e.memset(scrow[:], 0.0), writes=(b_scrow,))
    em.do("vector", lambda e: e.memset(tiny[:], 0.0), writes=(b_tiny,))

    finalize_max(m1c, b_m1c, NTILES * FC, 0, pp[1])
    cc_round([], [scale_pair(ccr[0:1, 0:1], b_ccr, SC_R1, SC_S1, pp[0])])

    if upto == "attn2":
        with nc.Block() as block:
            sems = {s: ec(nc.semaphore(s)) for s in em.counts}
            em.replay(block, sems)
        scopeB.close()
        st.close()
        return nc

    scopeB.close()
    snapB = em.snapshot()
    scopeC = ExitStack()

    def sbc(name, shape, dt):
        return sb(name, shape, dt, scope=scopeC, init=snapB)

    gmx, b_gmx = sbc("gmx", [128, NTILES * MI], BF16)
    umx, b_umx = sbc("umx", [128, NTILES * MI], BF16)
    tmx, b_tmx = sbc("tmx", [128, NTILES * MI], BF16)
    pmx, b_pmx = sbc("pmx", [128, NTILES * MI], BF16)
    pf = [ps(f"pf{i}", [128, TT], scope=scopeC, init=snapB) for i in range(8)]

    scopeB2 = ExitStack()

    def sbb(name, shape, dt):
        return sb(name, shape, dt, scope=scopeB2, init=snapB)

    fscr = [sbb(f"fscr{i}", [128, 2 * TT], F32) for i in range(2)]
    gins = [sbb(f"gin{i}", [128, 2 * TT], BF16) for i in range(2)]
    iscr = [sbb(f"iscr{i}", [128, 2 * TT], I16) for i in range(2)]
    uscr = [sbb(f"uscr{i}", [128, 2 * TT], U16) for i in range(2)]
    f3scr = [sbb(f"f3scr{i}", [128, 2 * TT], F32) for i in range(2)]
    p2c = [sbb(f"p2c{i}", [128, 2 * TT], F32) for i in range(2)]
    tbs = [sbb(f"tb{i}", [128, 2 * TT], BF16) for i in range(2)]
    wall = [sbb(f"wall{i}", [128, NTILES * TT], BF16) for i in range(2)]

    scopeC1 = ExitStack()
    xq, b_xq = sb("xq", [128, FC * BPC * L], BF16, scope=scopeC1, init=snapB)
    wgT, b_wgT = sb("wgT", [128, FC * DI], BF16, scope=scopeC1, init=snapB)
    wupT, b_wupT = wgT, b_wgT
    h2in = [sb(f"h2in{i}", [128, BPC * L // 4], BF16, scope=scopeC1, init=snapB) for i in range(2)]
    _wave_deps = b_wgq_dram.deps_read() + b_wgT.deps_write()
    for kc in range(FC):
        em.do("sync", lambda e, kc=kc: e.dma_start_transpose(
            out=wgT[:, kc * DI:(kc + 1) * DI], in_=wgateq_h[:, kc * 128:(kc + 1) * 128]),
            deps=_wave_deps, dma="d_wgt")
    em.freeze_wave((b_wgT,), "d_wgt")

    for fc_ in range(FC):
        for hh in range(4):
            sl2 = (fc_ * 4 + hh) % 2
            hi, b_hi = h2in[sl2]
            em.do("sync", lambda e, fc_=fc_, hh=hh, hi=hi: e.dma_start(
                out=hi[:], in_=h2_h[fc_, :, hh * 1024:(hh + 1) * 1024]),
                deps=b_h2_dram.deps_read(), writes=(b_hi,), dma=f"d_h2in{sl2}")
            fs, b_fs = fscr[hh % 2]
            em.do("vector", lambda e, hi=hi, fs=fs: e.tensor_scalar(
                fs[:, 0:1024], hi[:], scol(SC_R1), C_RNE,
                op0=ALU.mult, op1=ALU.add),
                reads=(b_hi, b_scl), writes=(b_fs,))
            em.do("vector", lambda e, fc_=fc_, hh=hh, fs=fs: e.tensor_scalar(
                xq[:, fc_ * BPC * L + hh * 1024:fc_ * BPC * L + (hh + 1) * 1024],
                fs[:, 0:1024], -C_RNE, scol(SC_S1), op0=ALU.add, op1=ALU.mult),
                reads=(b_fs, b_scl), rw=(b_xq,))

    def mm_block(wT, b_wT, tt, m, pt, b_pt):
        for kc in range(FC):
            em.do("tensor", lambda e, kc=kc, tt=tt, m=m, pt=pt, wT=wT: e.matmul(
                pt[:], wT[:, kc * DI + m * 128:kc * DI + (m + 1) * 128],
                xq[:, kc * BPC * L + tt * TT:kc * BPC * L + (tt + 1) * TT],
                start=(kc == 0), stop=(kc == FC - 1)),
                reads=(b_wT, b_xq), writes=(b_pt,) if kc == 0 else (), rw=(b_pt,) if kc else ())

    b_g_dram = Buf("g_d")
    b_u_dram = Buf("u_d")
    b_t_dram = Buf("t_d")
    dq_u = dq_by_w["u"]
    for m in range(MI):
        wa, b_wa = wall[m % 2]
        for kc in range(FC):
            for tt in range(NTILES):
                pt, b_pt = pf[tt]
                em.do("tensor", lambda e, kc=kc, tt=tt, m=m, pt=pt: e.matmul(
                    pt[:], wgT[:, kc * DI + m * 128:kc * DI + (m + 1) * 128],
                    xq[:, kc * BPC * L + tt * TT:kc * BPC * L + (tt + 1) * TT],
                    start=(kc == 0), stop=(kc == FC - 1)),
                    reads=(b_wgT, b_xq), writes=(b_pt,) if kc == 0 else (), rw=(b_pt,) if kc else ())
        for tt in range(NTILES):
            pt, b_pt = pf[tt]
            em.do("scalar", lambda e, pt=pt, wa=wa, tt=tt: e.copy(
                wa[:, tt * TT:(tt + 1) * TT], pt[:]), reads=(b_pt,), rw=(b_wa,))
        em.do("vector", lambda e, wa=wa, m=m: e.tensor_reduce(
            gmx[:, m * NTILES:(m + 1) * NTILES],
            wa[:].rearrange("p (a b) -> p a b", b=TT), axis=AX.X, op=ALU.max,
            apply_absolute_value=True), reads=(b_wa,), rw=(b_gmx,))
        em.do("gpsimd", lambda e, wa=wa, m=m: e.dma_start(
            out=g_h[:, m].rearrange("a p b -> p a b"),
            in_=wa[:].rearrange("p (a b) -> p a b", b=TT)),
            reads=(b_wa,), rw=(b_g_dram,), dma=f"d_wsg{m % 2}")
        if m * 2 < len(dq_u):
            emit_dq(dq_u[m * 2], m * 2)
            if m * 2 + 1 < len(dq_u):
                emit_dq(dq_u[m * 2 + 1], m * 2 + 1)
    b_wuq_dram = Buf("wuq_d")
    b_wuq_dram.wt = {"d_dqo_u0": em.counts.get("d_dqo_u0", 0),
                     "d_dqo_u1": em.counts.get("d_dqo_u1", 0)}
    _wave_depsu = b_wuq_dram.deps_read() + b_wgT.deps_write()
    for kc in range(FC):
        em.do("sync", lambda e, kc=kc: e.dma_start_transpose(
            out=wupT[:, kc * DI:(kc + 1) * DI], in_=wupq_h[:, kc * 128:(kc + 1) * 128]),
            deps=_wave_depsu, dma="d_wut")
    em.freeze_wave((b_wupT,), "d_wut")
    finalize_max(gmx, b_gmx, NTILES * MI, 1, pf[4])

    dq_d = dq_by_w["d"]

    def dq_d_range(lo, hi):
        def f():
            for s_ in range(lo, hi):
                emit_dq(dq_d[s_], s_)
        return f

    cc_round([], [dq_d_range(0, 6),
                  scale_pair(ccr[0:1, 1:2], b_ccr, SC_R3, SC_S3, pf[5])])

    for m in range(MI):
        wa, b_wa = wall[m % 2]
        for kc in range(FC):
            for tt in range(NTILES):
                pt, b_pt = pf[tt]
                em.do("tensor", lambda e, kc=kc, tt=tt, m=m, pt=pt: e.matmul(
                    pt[:], wupT[:, kc * DI + m * 128:kc * DI + (m + 1) * 128],
                    xq[:, kc * BPC * L + tt * TT:kc * BPC * L + (tt + 1) * TT],
                    start=(kc == 0), stop=(kc == FC - 1)),
                    reads=(b_wupT, b_xq), writes=(b_pt,) if kc == 0 else (), rw=(b_pt,) if kc else ())
        for tt in range(NTILES):
            pt, b_pt = pf[tt]
            if tt % 2 == 0:
                em.do("scalar", lambda e, pt=pt, wa=wa, tt=tt: e.copy(
                    wa[:, tt * TT:(tt + 1) * TT], pt[:]), reads=(b_pt,), rw=(b_wa,))
            else:
                em.do("vector", lambda e, pt=pt, wa=wa, tt=tt: e.tensor_scalar(
                    wa[:, tt * TT:(tt + 1) * TT], pt[:], 0.0, None, op0=ALU.add),
                    reads=(b_pt,), rw=(b_wa,))
        em.do("vector", lambda e, wa=wa, m=m: e.tensor_reduce(
            umx[:, m * NTILES:(m + 1) * NTILES],
            wa[:].rearrange("p (a b) -> p a b", b=TT), axis=AX.X, op=ALU.max,
            apply_absolute_value=True), reads=(b_wa,), rw=(b_umx,))
        em.do("gpsimd", lambda e, wa=wa, m=m: e.dma_start(
            out=u_h[:, m].rearrange("a p b -> p a b"),
            in_=wa[:].rearrange("p (a b) -> p a b", b=TT)),
            reads=(b_wa,), rw=(b_u_dram,), dma=f"d_wsu{m % 2}")
        for bi in range(4):
            tt0 = bi * 2
            sl = (m * 4 + bi) % 2
            gi, b_gi = gins[sl]
            em.do("sync", lambda e, gi=gi, tt0=tt0, m=m: e.dma_start(
                out=gi[:].rearrange("p (a b) -> p a b", b=TT),
                in_=g_h[tt0:tt0 + 2, m].rearrange("a p b -> p a b")),
                deps=b_g_dram.deps_read(), writes=(b_gi,), dma=f"d_gin{sl}")
            s1_, b_s1 = fscr[sl]
            em.do("scalar", lambda e, gi=gi, s1_=s1_: e.activation(
                s1_[:], gi[:], AF.Copy, bias=C_RNE, scale=scol(SC_R3)),
                reads=(b_gi, b_scl), writes=(b_s1,))
            gint, b_gint = iscr[sl]
            if bi % 2 == 0:
                em.do("scalar", lambda e, s1_=s1_, gint=gint: e.activation(
                    gint[:], s1_[:], AF.Copy, bias=-C_RNE, scale=1.0),
                    reads=(b_s1,), writes=(b_gint,))
            else:
                em.do("vector", lambda e, s1_=s1_, gint=gint: e.tensor_scalar(
                    gint[:], s1_[:], -C_RNE, None, op0=ALU.add),
                    reads=(b_s1,), writes=(b_gint,))
            sg_, b_sg = f3scr[sl]
            em.do("scalar", lambda e, gint=gint, sg_=sg_: e.activation(
                sg_[:], gint[:], AF.Sigmoid, bias=ZERO, scale=scol(SC_S3)),
                reads=(b_gint, b_scl, b_kv), writes=(b_sg,))
            pc_, b_pc = p2c[sl]
            em.do("scalar", lambda e, sg_=sg_, pc_=pc_: e.activation(
                pc_[:], sg_[:], AF.Copy, bias=C_RNE, scale=65536.0),
                reads=(b_sg,), writes=(b_pc,))
            su, b_su = uscr[sl]
            em.do("vector", lambda e, pc_=pc_, su=su: e.tensor_scalar(
                su[:], pc_[:], -C_RNE, 65535.0, op0=ALU.add, op1=ALU.min),
                reads=(b_pc,), writes=(b_su,))
            tb, b_tb = tbs[sl]
            em.do("vector", lambda e, gint=gint, su=su, tb=tb: e.tensor_tensor(
                tb[:], gint[:], su[:], op=ALU.mult), reads=(b_gint, b_su), writes=(b_tb,))
            em.do("vector", lambda e, tb=tb, m=m, tt0=tt0: e.tensor_reduce(
                tmx[:, m * NTILES + tt0:m * NTILES + tt0 + 2],
                tb[:].rearrange("p (a b) -> p a b", b=TT), axis=AX.X, op=ALU.max,
                apply_absolute_value=True), reads=(b_tb,), rw=(b_tmx,))
            em.do("gpsimd", lambda e, tb=tb, tt0=tt0, m=m: e.dma_start(
                out=t_h[tt0:tt0 + 2, m].rearrange("a p b -> p a b"),
                in_=tb[:].rearrange("p (a b) -> p a b", b=TT)),
                reads=(b_tb,), rw=(b_t_dram,), dma=f"d_wst{sl}")
    finalize_max(umx, b_umx, NTILES * MI, 2, pf[4])
    finalize_max(tmx, b_tmx, NTILES * MI, 3, pf[5])
    scopeC1.close()
    scopeB2.close()
    snapB2 = em.snapshot()
    tiW = [sb(f"tiW{i}", [128, 4 * TT], BF16, scope=scopeC, init=snapB2) for i in range(2)]
    uiW = [sb(f"uiW{i}", [128, 4 * TT], BF16, scope=scopeC, init=snapB2) for i in range(2)]
    s1W = [sb(f"s1W{i}", [128, 4 * TT], F32, scope=scopeC, init=snapB2) for i in range(2)]
    tintW = [sb(f"tintW{i}", [128, 4 * TT], I16, scope=scopeC, init=snapB2) for i in range(2)]
    s2W = [sb(f"s2W{i}", [128, 4 * TT], F32, scope=scopeC, init=snapB2) for i in range(2)]
    uintW = [sb(f"uintW{i}", [128, 4 * TT], I16, scope=scopeC, init=snapB2) for i in range(2)]
    pbW = s2W
    s3W = s1W

    SC_S4 = 12
    SC_R4 = 13
    SC_S42 = 14

    def post_ar3():
        em.do("vector", lambda e: e.tensor_scalar(
            scrow[0:1, SC_S2:SC_S2 + 1], ccr[0:1, 2:3], 1.0 / 32767.0, 1e-12,
            op0=ALU.mult, op1=ALU.max), reads=(b_ccr,), rw=(b_scrow,))
        em.do("vector", lambda e: e.reciprocal(
            scrow[0:1, SC_R2:SC_R2 + 1], scrow[0:1, SC_S2:SC_S2 + 1]), rw=(b_scrow,))
        em.do("vector", lambda e: e.tensor_scalar(
            scrow[0:1, SC_S4:SC_S4 + 1], ccr[0:1, 3:4], scrow[0:1, SC_S3:SC_S3 + 1],
            2.0 ** -16, op0=ALU.mult, op1=ALU.mult), reads=(b_ccr,), rw=(b_scrow,))
        em.do("vector", lambda e: e.tensor_scalar(
            scrow[0:1, SC_S4:SC_S4 + 1], scrow[0:1, SC_S4:SC_S4 + 1], 1.0 / 32767.0,
            1e-12, op0=ALU.mult, op1=ALU.max), rw=(b_scrow,))
        em.do("vector", lambda e: e.reciprocal(
            scrow[0:1, SC_R4:SC_R4 + 1], scrow[0:1, SC_S4:SC_S4 + 1]), rw=(b_scrow,))
        em.do("vector", lambda e: e.tensor_scalar(
            scrow[0:1, SC_K1:SC_K1 + 1], scrow[0:1, SC_R4:SC_R4 + 1],
            scrow[0:1, SC_S3:SC_S3 + 1], 2.0 ** -16, op0=ALU.mult, op1=ALU.mult),
            rw=(b_scrow,))
        em.do("vector", lambda e: e.tensor_scalar(
            scrow[0:1, SC_S42:SC_S42 + 1], scrow[0:1, SC_S4:SC_S4 + 1],
            scrow[0:1, SC_S2:SC_S2 + 1], None, op0=ALU.mult), rw=(b_scrow,))
        bcast_scales(pf[5])

    cc_round([], [dq_d_range(6, 48), post_ar3])

    def p3_chain(tt, mb, and_pq=None):
        m0 = mb * 4
        sl = (tt * 6 + mb) % 2
        qsl = (tt * 6 + mb) % 3
        ti_, b_ti = tiW[sl]
        ui_, b_ui = uiW[sl]
        em.do("gpsimd", lambda e, ti_=ti_, tt=tt, m0=m0: e.dma_start(
            out=ti_[:].rearrange("p (a b) -> p a b", b=TT),
            in_=t_h[tt, m0:m0 + 4].rearrange("a p b -> p a b")),
            deps=b_t_dram.deps_read(), writes=(b_ti,), dma=f"d_gti{qsl}")
        em.do("gpsimd", lambda e, ui_=ui_, tt=tt, m0=m0: e.dma_start(
            out=ui_[:].rearrange("p (a b) -> p a b", b=TT),
            in_=u_h[tt, m0:m0 + 4].rearrange("a p b -> p a b")),
            deps=b_u_dram.deps_read(), writes=(b_ui,), dma=f"d_gui{qsl}")
        s1_, b_s1 = s1W[sl]
        em.do("scalar", lambda e, ti_=ti_, s1_=s1_: e.activation(
            s1_[:], ti_[:], AF.Copy, bias=C_RNE, scale=scol(SC_K1)),
            reads=(b_ti, b_scl), writes=(b_s1,))
        tint, b_tint = tintW[sl]
        em.do("scalar", lambda e, s1_=s1_, tint=tint: e.activation(
            tint[:], s1_[:], AF.Copy, bias=-C_RNE, scale=1.0),
            reads=(b_s1,), writes=(b_tint,))
        s2_, b_s2 = s2W[sl]
        em.do("scalar", lambda e, ui_=ui_, s2_=s2_: e.activation(
            s2_[:], ui_[:], AF.Copy, bias=C_RNE, scale=scl[:, SC_R2:SC_R2 + 1]),
            reads=(b_ui, b_scl), writes=(b_s2,))
        uint, b_uint = uintW[sl]
        em.do("vector", lambda e, s2_=s2_, uint=uint: e.tensor_scalar(
            uint[:], s2_[:], -C_RNE, None, op0=ALU.add), reads=(b_s2,), writes=(b_uint,))
        pb, b_pb = pbW[sl]
        em.do("vector", lambda e, tint=tint, uint=uint, pb=pb: e.tensor_tensor(
            pb[:], tint[:], uint[:], op=ALU.mult), reads=(b_tint, b_uint), writes=(b_pb,))
        if and_pq is None:
            idx0 = tt * MI + m0
            em.do("vector", lambda e, pb=pb, idx0=idx0: e.tensor_reduce(
                pmx[:, idx0:idx0 + 4], pb[:].rearrange("p (a b) -> p a b", b=TT),
                axis=AX.X, op=ALU.max, apply_absolute_value=True),
                reads=(b_pb,), rw=(b_pmx,))
        else:
            pq, b_pq = and_pq
            s3_, b_s3 = s3W[sl]
            em.do("vector", lambda e, pb=pb, s3_=s3_: e.tensor_scalar(
                s3_[:], pb[:], scol(SC_K3), C_RNE, op0=ALU.mult, op1=ALU.add),
                reads=(b_pb, b_scl), writes=(b_s3,))
            em.do("vector", lambda e, s3_=s3_, m0=m0, pq=pq: e.tensor_scalar(
                pq[:, m0 * TT:(m0 + 4) * TT], s3_[:], -C_RNE, scol(SC_S5),
                op0=ALU.add, op1=ALU.mult),
                reads=(b_s3, b_scl), rw=(b_pq,))

    for tt in range(NTILES):
        for mb in range(6):
            p3_chain(tt, mb)
    finalize_max(pmx, b_pmx, NTILES * MI, 0, pf[4])

    def post_ar4():
        em.do("vector", lambda e: e.tensor_scalar(
            scrow[0:1, SC_S5:SC_S5 + 1], ccr[0:1, 0:1], scrow[0:1, SC_S42:SC_S42 + 1],
            None, op0=ALU.mult), reads=(b_ccr,), rw=(b_scrow,))
        em.do("vector", lambda e: e.tensor_scalar(
            scrow[0:1, SC_S5:SC_S5 + 1], scrow[0:1, SC_S5:SC_S5 + 1], 1.0 / 32767.0,
            1e-12, op0=ALU.mult, op1=ALU.max), rw=(b_scrow,))
        em.do("vector", lambda e: e.reciprocal(
            scrow[0:1, 15:16], scrow[0:1, SC_S5:SC_S5 + 1]), rw=(b_scrow,))
        em.do("vector", lambda e: e.tensor_scalar(
            scrow[0:1, SC_K3:SC_K3 + 1], scrow[0:1, 15:16],
            scrow[0:1, SC_S42:SC_S42 + 1], None, op0=ALU.mult), rw=(b_scrow,))
        bcast_scales(pf[5])

    cc_round([], [post_ar4])

    snapC = em.snapshot()
    pqs = [sb(f"pq{i}", [128, MI * TT], BF16, scope=scopeC, init=snapC) for i in range(2)]
    wdT, b_wdT = sb("wdT", [128, MI * 1024], BF16, scope=scopeC, init=snapC)
    xmin = [sb(f"xmin{i}", [128, TT], F32, scope=scopeC, init=snapC) for i in range(2)]
    otile = [sb(f"ot{i}", [128, TT], F32, scope=scopeC, init=snapC) for i in range(2)]
    b_wdq_dram = Buf("wdq_d")
    b_wdq_dram.wt = {"d_dqo_d0": em.counts.get("d_dqo_d0", 0),
                     "d_dqo_d1": em.counts.get("d_dqo_d1", 0)}
    _wave_deps2 = b_wdq_dram.deps_read() + b_wdT.deps_write()
    for kcd in range(MI):
        em.do("sync", lambda e, kcd=kcd: e.dma_start_transpose(
            out=wdT[:, kcd * 1024:(kcd + 1) * 1024], in_=wdownq_h[:, kcd * 128:(kcd + 1) * 128]),
            deps=_wave_deps2, dma="d_wdt")
    em.freeze_wave((b_wdT,), "d_wdt")

    out_toks = []

    for mb in range(6):
        p3_chain(0, mb, and_pq=pqs[0])
    for tt in range(NTILES):
        bb = tt // TPS
        tok0 = tt * TT
        pq, b_pq = pqs[tt % 2]
        if tt + 1 < NTILES:
            for mb in range(6):
                p3_chain(tt + 1, mb, and_pq=pqs[(tt + 1) % 2])
        for mc in range(FC):
            if mc == 0:
                xi2, b_xi2 = xmin[0]
                em.do("sync", lambda e, xi2=xi2, tok0=tok0: e.dma_start(
                    out=xi2[:], in_=xmid_h[0, :, tok0:tok0 + TT]),
                    deps=b_xmid_dram.deps_read(), writes=(b_xi2,), dma="d_xmin0")
            pt, b_pt = pf[mc % 6]
            for kcd in range(MI):
                em.do("tensor", lambda e, pt=pt, mc=mc, kcd=kcd, pq=pq: e.matmul(
                    pt[:], wdT[:, kcd * 1024 + mc * 128:kcd * 1024 + (mc + 1) * 128],
                    pq[:, kcd * TT:(kcd + 1) * TT],
                    start=(kcd == 0), stop=(kcd == MI - 1)),
                    reads=(b_wdT, b_pq), writes=(b_pt,) if kcd == 0 else (), rw=(b_pt,) if kcd else ())
            xi, b_xi = xmin[mc % 2]
            if mc + 1 < FC:
                xi3, b_xi3 = xmin[(mc + 1) % 2]
                em.do("sync", lambda e, xi3=xi3, mc=mc, tok0=tok0: e.dma_start(
                    out=xi3[:], in_=xmid_h[mc + 1, :, tok0:tok0 + TT]),
                    deps=b_xmid_dram.deps_read(), writes=(b_xi3,), dma=f"d_xmin{(mc + 1) % 2}")
            ot, b_ot = otile[mc % 2]
            em.do("vector", lambda e, pt=pt, xi=xi, ot=ot, mc=mc, bb=bb: e.scalar_tensor_tensor(
                ot[:], pt[:], gb_ap(mc, bb), xi[:], op0=ALU.mult, op1=ALU.add),
                reads=(b_pt, b_modg, b_xi), writes=(b_ot,))
            tk = em.do("sync", lambda e, ot=ot, mc=mc, tok0=tok0: e.dma_start(
                out=out_h[mc, :, tok0:tok0 + TT], in_=ot[:]), reads=(b_ot,), dma=f"d_out{mc % 2}")
            out_toks.append(tk)

    if "ffn" in taps:
        tg_ = nc.dram_tensor("tap_g", [NTILES, MI, 128, TT], BF16, kind="ExternalOutput").ap()
        tu_ = nc.dram_tensor("tap_u", [NTILES, MI, 128, TT], BF16, kind="ExternalOutput").ap()
        tt_ = nc.dram_tensor("tap_t", [NTILES, MI, 128, TT], BF16, kind="ExternalOutput").ap()
        ts_ = nc.dram_tensor("tap_scl", [128, 12], F32, kind="ExternalOutput").ap()
        tq_ = nc.dram_tensor("tap_pq", [128, MI * TT], BF16, kind="ExternalOutput").ap()
        k1 = em.do("sync", lambda e: e.dma_start(out=tg_, in_=g_h), deps=b_g_dram.deps_read(), dma="d_tap")
        k2 = em.do("sync", lambda e: e.dma_start(out=tu_, in_=u_h), deps=b_u_dram.deps_read(), dma="d_tap")
        k3 = em.do("sync", lambda e: e.dma_start(out=tt_, in_=t_h), deps=b_t_dram.deps_read(), dma="d_tap")
        k4 = em.do("sync", lambda e: e.dma_start(out=ts_, in_=scl[:]), reads=(b_scl,), dma="d_tap")
        k5 = em.do("sync", lambda e: e.dma_start(out=tq_, in_=pqs[1][0][:]), reads=(pqs[1][1],), dma="d_tap")
        out_toks += [k1, k2, k3, k4, k5]

    em._push("sync", None, out_toks, None, 0)

    with nc.Block() as block:
        sems = {s_: ec(nc.semaphore(s_)) for s_ in em.counts}
        em.replay(block, sems)
    scopeC.close()
    st.close()
    return nc


def _host_consts():
    triA = np.triu(np.full((128, 128), -1.0 / 16.0, np.float32)).astype(BF)
    triR = np.tril(np.full((128, 128), -1.0 / 16.0, np.float32), -1).astype(BF)
    mask = np.triu(np.ones((128, 128), np.float32)).astype(BF)
    ones_ln = np.full((128, 128), 2.0 ** -10, np.float32).astype(BF)
    bd = np.zeros((128, 128), np.float32)
    bd[0:64, 0:64] = 2.0 ** -6
    bd[64:128, 64:128] = 2.0 ** -6
    bd_rms = bd.astype(BF)
    tri = np.concatenate([triA, triR, mask], axis=1)
    ones = np.concatenate([ones_ln, bd_rms], axis=1)
    kvec = np.tile(np.array([[0.0, EPS, LN8, 1.0]], np.float32), (128, 1))
    mask8 = np.tile(mask, (1, 8)).copy()
    onerow = np.ones((1, TT), np.float32).astype(BF)
    bone = np.ones((1, 128), np.float32)
    idn = np.eye(128, dtype=np.float32)
    return tri, ones, kvec, mask8, onerow, bone, idn


def kernel(**inputs):
    x = np.asarray(inputs["L0_attn_x"], np.float32)
    tri, ones, kvec, mask8, onerow, bone, idn = _host_consts()
    gnw = np.tile(np.asarray(inputs["L0_attn_g_norm_w"], np.float32), 2)[:, None].copy()
    wg2a = np.concatenate([np.asarray(inputs["L0_attn_Wg2"], np.float32),
                           np.asarray(inputs["L0_attn_bg2"], np.float32)[None, :]], axis=0)
    cfull = np.asarray(inputs["L0_c"], np.float32)

    nc = build(NCORES)
    in_maps = []
    for core in range(NCORES):
        bs = slice(core * BPC, (core + 1) * BPC)
        xs = x[bs]
        x_fm = np.ascontiguousarray(
            xs.transpose(2, 0, 1).reshape(FC, 128, BPC * L))
        ct = np.ascontiguousarray(
            cfull[bs].T.reshape(FC, 128, BPC))
        in_maps.append({
            "x": x_fm,
            "wq": np.asarray(inputs["L0_attn_Wq"], np.float32),
            "wk": np.asarray(inputs["L0_attn_Wk"], np.float32),
            "wv": np.asarray(inputs["L0_attn_Wv"], np.float32),
            "wgp": np.asarray(inputs["L0_attn_Wg"], np.float32),
            "wo": np.asarray(inputs["L0_attn_Wo"], np.float32),
            "wg1": np.asarray(inputs["L0_attn_Wg1"], np.float32),
            "wg2a": wg2a,
            "gnw": gnw,
            "wup": np.asarray(inputs["L0_ffn_w_up"], np.float32),
            "wgate": np.asarray(inputs["L0_ffn_w_gate"], np.float32),
            "wdown": np.asarray(inputs["L0_ffn_w_down"], np.float32),
            "ct": ct,
            "wmod": np.asarray(inputs["L0_wmod"], np.float32),
            "tri": tri, "onesc": ones, "kvec": kvec, "mask8": mask8, "onerow": onerow,
            "bone": bone, "idn": idn,
        })
    res = run_bass_kernel_spmd(nc, in_maps, core_ids=list(range(NCORES)))
    out = np.empty((B, L, DM), np.float32)
    for core in range(NCORES):
        o = res.results[core]["out"]
        o = o.reshape(DM, BPC, L).transpose(1, 2, 0)
        out[core * BPC:(core + 1) * BPC] = o
    return out


# version 19
# speedup vs baseline: 1.0122x; 1.0122x over previous
import math
from contextlib import ExitStack

import numpy as np
import ml_dtypes

import bass_rust
import concourse.bass as bass
import concourse.mybir as mybir
from concourse import library_config
import concourse.bass_utils as _bass_utils
from concourse.bass_utils import run_bass_kernel_spmd

if not getattr(_bass_utils, "_ldw_opt_patched", False):
    _orig_run_command = _bass_utils.run_command

    def _run_command_ldw(cmd, *a, **k):
        cmd = [("--enable-ldw-opt=true" if c == "--enable-ldw-opt=false" else c)
               for c in cmd]
        return _orig_run_command(cmd, *a, **k)

    _bass_utils.run_command = _run_command_ldw
    _bass_utils._ldw_opt_patched = True

F32 = mybir.dt.float32
BF16 = mybir.dt.bfloat16
I16 = mybir.dt.int16
U16 = mybir.dt.uint16
AF = mybir.ActivationFunctionType
ALU = mybir.AluOpType
AX = mybir.AxisListType
RED = bass_rust.ReduceOp

C_RNE = 12582912.0
NCORES = 8
B, L, DM = 16, 2048, 1024
H, D = 16, 64
BPC = B // NCORES
FC = DM // 128
TT = 512
TPS = L // TT
NTILES = BPC * TPS
TC = 128
NTC = TT // TC
HP = 8
DI = 3072
MI = DI // 128
QMAXA = 32767.0
EPS = 1e-5
LN8 = float(np.log(0.125))

BF = ml_dtypes.bfloat16


ENGINES = ("tensor", "vector", "scalar", "gpsimd", "sync")
CSEM = {"tensor": "st", "vector": "sv", "scalar": "ss", "gpsimd": "sg"}


def _merge(deps):
    m = {}
    for d in deps:
        if d is None:
            continue
        s, v = d
        if m.get(s, -1) < v:
            m[s] = v
    return m


class Buf:

    __slots__ = ("wt", "rt", "name")

    def __init__(self, name=""):
        self.wt = {}
        self.rt = {}
        self.name = name

    def deps_read(self):
        return list(self.wt.items())

    def deps_write(self):
        return list(self.wt.items()) + list(self.rt.items())


class Em:

    def __init__(self):
        self.steps = {e: [] for e in ENGINES}
        self.counts = {s: 0 for s in CSEM.values()}
        self.counts["s_cc"] = 0

    def sem(self, name):
        if name not in self.counts:
            self.counts[name] = 0
        return name

    LABEL = [None]

    def _push(self, eng, fn, deps, sem, inc):
        m = _merge(deps)
        lbl = Em.LABEL[0]
        if sem is not None:
            self.counts[sem] += inc
            tok = (sem, self.counts[sem])
        else:
            tok = None
        self.steps[eng].append((m, fn, sem, inc, lbl))
        return tok

    def do(self, eng, fn, reads=(), writes=(), rw=(), deps=(), dma=None, cc=False):
        d = list(deps)
        for b in reads:
            d += b.deps_read()
        for b in writes:
            d += b.deps_write()
        for b in rw:
            d += b.deps_write()
        if cc:
            sem, inc = "s_cc", 1
        elif dma is not None:
            sem, inc = self.sem(dma), 16
        else:
            sem, inc = CSEM[eng], 1
        tok = self._push(eng, fn, d, sem, inc)
        s, v = tok
        for b in reads:
            if b.rt.get(s, -1) < v:
                b.rt[s] = v
        for b in writes:
            b.wt = {s: v}
            b.rt = {}
        for b in rw:
            if b.wt.get(s, -1) < v:
                b.wt[s] = v
        return tok

    def snapshot(self):
        return {k: v for k, v in self.counts.items() if v > 0}

    def freeze_wave(self, bufs, semname):
        tot = self.counts[semname]
        for b in bufs:
            b.wt = {semname: tot}
        return (semname, tot)

    def replay(self, block, sems):
        for eng in ENGINES:
            steps = self.steps[eng]
            if not steps:
                continue
            sec = getattr(block, eng)

            own = CSEM.get(eng)

            def _body(e, _steps=steps, _own=own):
                credit = {}
                for deps, fn, sem, inc, lbl in _steps:
                    for s, v in deps.items():
                        if s == _own and eng == "tensor":
                            continue
                        if credit.get(s, 0) < v:
                            e.wait_ge(sems[s], v)
                            credit[s] = v
                    if fn is not None:
                        inst = fn(e)
                        if sem is not None:
                            inst.then_inc(sems[sem], inc)

            sec(_body)


def bslice(t, bi, n):
    return t[:, bi * n:(bi + 1) * n]


def build(n_cores=NCORES, taps=(), upto="all"):
    nc = bass.Bass("TRN2", target_bir_lowering=False, debug=False)
    em = Em()
    taps = set(taps)
    tap_bufs = {}

    x_h = nc.dram_tensor("x", [FC, 128, BPC * L], F32, kind="ExternalInput").ap()
    wq_h = nc.dram_tensor("wq", [DM, DM], F32, kind="ExternalInput").ap()
    wk_h = nc.dram_tensor("wk", [DM, DM], F32, kind="ExternalInput").ap()
    wv_h = nc.dram_tensor("wv", [DM, DM], F32, kind="ExternalInput").ap()
    wgp_h = nc.dram_tensor("wgp", [DM, DM], F32, kind="ExternalInput").ap()
    wo_h = nc.dram_tensor("wo", [DM, DM], F32, kind="ExternalInput").ap()
    wg1_h = nc.dram_tensor("wg1", [DM, 16], F32, kind="ExternalInput").ap()
    wg2a_h = nc.dram_tensor("wg2a", [17, DM], F32, kind="ExternalInput").ap()
    gnw_h = nc.dram_tensor("gnw", [128, 1], F32, kind="ExternalInput").ap()
    wup_h = nc.dram_tensor("wup", [DI, DM], F32, kind="ExternalInput").ap()
    wgate_h = nc.dram_tensor("wgate", [DI, DM], F32, kind="ExternalInput").ap()
    wdown_h = nc.dram_tensor("wdown", [DM, DI], F32, kind="ExternalInput").ap()
    ct_h = nc.dram_tensor("ct", [FC, 128, BPC], F32, kind="ExternalInput").ap()
    wmod_h = nc.dram_tensor("wmod", [DM, 6 * DM], F32, kind="ExternalInput").ap()
    tri_h = nc.dram_tensor("tri", [128, 3 * 128], BF16, kind="ExternalInput").ap()
    ones_h = nc.dram_tensor("onesc", [128, 2 * 128], BF16, kind="ExternalInput").ap()
    kv_h = nc.dram_tensor("kvec", [128, 4], F32, kind="ExternalInput").ap()
    mask8_h = nc.dram_tensor("mask8", [128, 1024], BF16, kind="ExternalInput").ap()
    onerow_h = nc.dram_tensor("onerow", [1, TT], BF16, kind="ExternalInput").ap()
    bone_h = nc.dram_tensor("bone", [1, 128], F32, kind="ExternalInput").ap()
    idn_h = nc.dram_tensor("idn", [128, 128], F32, kind="ExternalInput").ap()
    out_h = nc.dram_tensor("out", [FC, 128, BPC * L], F32, kind="ExternalOutput").ap()

    xmid_h = nc.dram_tensor("xmid", [FC, 128, BPC * L], F32).ap()
    h2_h = nc.dram_tensor("h2d", [FC, 128, BPC * L], BF16).ap()
    g_h = nc.dram_tensor("gspill", [NTILES, MI, 128, TT], BF16).ap()
    u_h = nc.dram_tensor("uspill", [NTILES, MI, 128, TT], BF16).ap()
    t_h = nc.dram_tensor("tspill", [NTILES, MI, 128, TT], BF16).ap()
    wupq_h = nc.dram_tensor("wupq", [DI, DM], BF16).ap()
    wgateq_h = nc.dram_tensor("wgateq", [DI, DM], BF16).ap()
    wdownq_h = nc.dram_tensor("wdownq", [DM, DI], BF16).ap()
    cc_in_h = nc.dram_tensor("cc_in", [1, 4], F32).ap()
    cc_out_h = nc.dram_tensor("cc_out", [1, 4], F32).ap()

    st = ExitStack()
    ec = st.enter_context

    def sb(name, shape, dt, scope=None, init=None):
        t = (scope or st).enter_context(nc.sbuf_tensor(name, shape, dt))
        b = Buf(name)
        if init:
            b.wt = dict(init)
        return t, b

    def ps(name, shape, scope=None, init=None):
        t = (scope or st).enter_context(nc.psum_tensor(name, shape, F32))
        b = Buf(name)
        if init:
            b.wt = dict(init)
        return t, b

    kv, b_kv = sb("kv", [128, 4], F32)
    ZERO, EPSC, LN8C, ONEC = (kv[:, i:i + 1] for i in range(4))
    gnw, b_gnw = sb("gnwt", [128, 1], F32)
    tri, b_tri = sb("trit", [128, 3 * 128], BF16)
    triA, triR, MASK = (bslice(tri, i, 128) for i in range(3))
    onesc, b_ones = sb("onest", [128, 2 * 128], BF16)
    ONES_LN, BD_RMS = bslice(onesc, 0, 128), bslice(onesc, 1, 128)
    modr, b_modr = sb("modr", [128, 96], F32)
    modg, b_modg = sb("modg", [128, 32], F32)
    modc, b_modc = sb("modc", [128, 96], F32)
    silc, b_silc = sb("silc", [128, FC * BPC], F32)
    ctb, b_ct = sb("ctb", [128, FC * BPC], F32)
    sscr, b_sscr = sb("sscr", [128, 16], F32)
    scl, b_scl = sb("scl", [128, 12], F32)
    ccs, b_ccs = sb("ccs", [1, 4], F32)
    ccr, b_ccr = sb("ccr", [1, 4], F32)
    tiny, b_tiny = sb("tiny", [128, 8], F32)
    dq_in = [sb(f"dq_in{i}", [128, 512], F32) for i in range(2)]
    dq_s, b_dqs = sb("dq_s", [128, 16], F32)
    dq_r, b_dqr = sb("dq_r", [128, 16], F32)
    dq_wr, b_dqwr = sb("dq_wr", [128, 512], F32)
    dq_o = [sb(f"dq_o{i}", [128, 512], BF16) for i in range(2)]

    idn, b_idn = sb("idn_s", [128, 128], F32)
    scrow, b_scrow = sb("scrow", [1, 16], F32)
    bone, b_bone = sb("bone_s", [1, 128], F32)

    def modcol(t, ch, bb):
        return t[:, (ch * BPC + bb):(ch * BPC + bb) + 1]

    def ga_ap(fc_, bb):
        return modg[:, (fc_ * BPC + bb):(fc_ * BPC + bb) + 1]

    def gb_ap(fc_, bb):
        return modg[:, 16 + (fc_ * BPC + bb):16 + (fc_ * BPC + bb) + 1]

    def sa_ap(fc_, bb):
        return modcol(modr, 0 * 8 + fc_, bb)

    def sb_ap(fc_, bb):
        return modcol(modr, 3 * 8 + fc_, bb)

    def ca1_ap(fc_, bb):
        return modcol(modc, 1 * 8 + fc_, bb)

    def cb1_ap(fc_, bb):
        return modcol(modc, 4 * 8 + fc_, bb)

    em.do("sync", lambda e: e.dma_start(out=kv[:], in_=kv_h), writes=(b_kv,), dma="d_c")
    em.do("sync", lambda e: e.dma_start(out=gnw[:], in_=gnw_h), writes=(b_gnw,), dma="d_c")
    em.do("sync", lambda e: e.dma_start(out=tri[:], in_=tri_h), writes=(b_tri,), dma="d_c")
    em.do("sync", lambda e: e.dma_start(out=onesc[:], in_=ones_h), writes=(b_ones,), dma="d_c")
    em.do("sync", lambda e: e.dma_start(out=ctb[:].rearrange("p (a b) -> p a b", a=FC),
          in_=ct_h.rearrange("a p b -> p a b")), writes=(b_ct,), dma="d_c")
    em.do("sync", lambda e: e.dma_start(out=bone[:], in_=bone_h), writes=(b_bone,), dma="d_c")
    em.do("sync", lambda e: e.dma_start(out=idn[:], in_=idn_h), writes=(b_idn,), dma="d_c")
    em.freeze_wave((b_kv, b_gnw, b_tri, b_ones, b_ct, b_bone, b_idn), "d_c")

    scopeA = ExitStack()
    e1, b_e1 = sb("m_e1", [128, FC * BPC], F32, scope=scopeA)
    em.do("scalar", lambda e: e.activation(e1[:], ctb[:], AF.Exp, bias=ZERO, scale=-1.0),
          reads=(b_ct, b_kv), writes=(b_e1,))
    em.do("vector", lambda e: e.tensor_scalar(e1[:], e1[:], 1.0, None, op0=ALU.add),
          rw=(b_e1,))
    em.do("vector", lambda e: e.reciprocal(e1[:], e1[:]), rw=(b_e1,))
    em.do("vector", lambda e: e.tensor_tensor(silc[:], ctb[:], e1[:], op=ALU.mult),
          reads=(b_ct, b_e1), writes=(b_silc,))

    wmb = []
    for i in range(2):
        t, b = sb(f"wmb{i}", [128, 1536], F32, scope=scopeA)
        wmb.append((t, b))
    macc, b_macc = sb("modacc", [128, 24], F32, scope=scopeA)
    mps, b_mps = ps("mod_ps", [128, 64], scope=scopeA)
    for cg in range(4):
        for kc in range(FC):
            slot = (cg * FC + kc) % 2
            t, bwm = wmb[slot]
            em.do("sync", lambda e, t=t, kc=kc, cg=cg: e.dma_start(
                out=t[:], in_=wmod_h[kc * 128:(kc + 1) * 128, cg * 1536:(cg + 1) * 1536]),
                writes=(bwm,), dma=f"d_wm{slot}")
            for mc in range(12):
                em.do("tensor", lambda e, t=t, kc=kc, mc=mc: e.matmul(
                    mps[:, mc * BPC:(mc + 1) * BPC], t[:, mc * 128:(mc + 1) * 128],
                    silc[:, kc * BPC:(kc + 1) * BPC],
                    start=True, stop=True),
                    reads=(bwm, b_silc),
                    writes=(b_mps,) if mc == 0 else (), rw=(b_mps,) if mc != 0 else ())
            if kc == 0:
                em.do("vector", lambda e: e.tensor_scalar(
                    macc[:], mps[:, 0:24], 0.0, None, op0=ALU.add),
                    reads=(b_mps,), writes=(b_macc,))
            else:
                em.do("vector", lambda e: e.tensor_tensor(
                    macc[:], macc[:], mps[:, 0:24], op=ALU.add),
                    reads=(b_mps, b_macc), writes=(b_macc,))
        em.do("scalar", lambda e, cg=cg: e.copy(modr[:, cg * 24:(cg + 1) * 24], macc[:]),
              reads=(b_macc,), rw=(b_modr,))
    for gi, j in enumerate((2, 5)):
        src = modr[:, j * 8 * BPC:(j + 1) * 8 * BPC]
        dst = modg[:, gi * 16:(gi + 1) * 16]
        em.do("vector", lambda e, s=src: e.tensor_scalar(sscr[:, 0:16], s, 0.0, None, op0=ALU.is_ge),
              reads=(b_modr,), writes=(b_sscr,))
        em.do("vector", lambda e: e.tensor_scalar(sscr[:, 0:16], sscr[:, 0:16], 2.0, -1.0, op0=ALU.mult, op1=ALU.add),
              rw=(b_sscr,))
        em.do("scalar", lambda e, s=src, d=dst: e.activation(d, s, AF.Abs, bias=ZERO, scale=1.0),
              reads=(b_modr, b_kv), rw=(b_modg,))
        em.do("vector", lambda e, d=dst: e.tensor_scalar(d, d, 0.25, None, op0=ALU.max),
              rw=(b_modg,))
        em.do("vector", lambda e, d=dst: e.tensor_tensor(d, d, sscr[:, 0:16], op=ALU.mult),
              reads=(b_sscr,), rw=(b_modg,))
    for j in (1, 4):
        em.do("vector", lambda e, j=j: e.tensor_scalar(
            modc[:, j * 16:(j + 1) * 16], modr[:, j * 16:(j + 1) * 16], 1.0, None, op0=ALU.add),
            reads=(b_modr,), rw=(b_modc,))

    scopeA.close()
    snapA = em.snapshot()

    if "mod" in taps:
        tp = nc.dram_tensor("tap_mod", [128, 96], F32, kind="ExternalOutput").ap()
        tg = nc.dram_tensor("tap_modg", [128, 32], F32, kind="ExternalOutput").ap()
        tk1 = em.do("sync", lambda e: e.dma_start(out=tp, in_=modr[:]), reads=(b_modr,), dma="d_tap")
        tk2 = em.do("sync", lambda e: e.dma_start(out=tg, in_=modg[:]), reads=(b_modg,), dma="d_tap")
        em._push("sync", None, [tk1, tk2], None, 0)


    if upto == "mod":
        block = ec(nc.Block())
        sems = {s: ec(nc.semaphore(s)) for s in em.counts}
        em.replay(block, sems)
        st.close()
        return nc

    scopeB = ExitStack()

    def sba(name, shape, dt):
        return sb(name, shape, dt, scope=scopeB, init=snapA)

    mask8, b_mask8 = sba("mask8t", [128, 1024], BF16)
    em.do("sync", lambda e: e.dma_start(out=mask8[:], in_=mask8_h), writes=(b_mask8,), dma="d_c2")
    em.freeze_wave((b_mask8,), "d_c2")

    wqs, b_wq = sba("wqs", [128, FC * DM], BF16)
    wks, b_wk = sba("wks", [128, FC * DM], BF16)
    wvs, b_wv = sba("wvs", [128, FC * DM], BF16)
    wgs, b_wg = sba("wgs", [128, FC * DM], BF16)
    wos, b_wo = sba("wos", [128, FC * DM], BF16)
    wg1s, b_wg1 = sba("wg1s", [128, FC * 16], BF16)
    wg2s, b_wg2 = sba("wg2s", [32, DM], BF16)
    for wsb, bw, wh in ((wqs, b_wq, wq_h), (wks, b_wk, wk_h), (wvs, b_wv, wv_h),
                        (wgs, b_wg, wgp_h), (wos, b_wo, wo_h)):
        em.do("gpsimd", lambda e, o=wsb, i=wh: e.dma_start(
            out=o[:].rearrange("p (a m) -> p a m", a=FC),
            in_=i.rearrange("(a p) m -> p a m", p=128)), writes=(bw,), dma="d_wg")
    em.do("gpsimd", lambda e: e.dma_start(
        out=wg1s[:].rearrange("p (a m) -> p a m", a=FC),
        in_=wg1_h.rearrange("(a p) m -> p a m", p=128)), writes=(b_wg1,), dma="d_wg")
    em.do("gpsimd", lambda e: e.dma_start(out=wg2s[0:17, :], in_=wg2a_h), writes=(b_wg2,), dma="d_wg")
    em.freeze_wave((b_wq, b_wk, b_wv, b_wg, b_wo, b_wg1, b_wg2), "d_wg")

    xb, b_xb = sba("xb", [128, FC * TT], BF16)
    x2s = [sba(f"x2s{i}", [128, TT], BF16) for i in range(2)]
    mu_bf, b_mu = sba("mu_bf", [128, TT], BF16)
    mu2, b_mu2 = sba("mu2", [128, TT], BF16)
    varr, b_varr = sba("varr", [128, TT], BF16)
    rstd_bf, b_rstd = sba("rstd_bf", [128, TT], BF16)
    xc1W, b_xc1W = sba("xc1W", [128, FC * TT], BF16)
    hsb, b_h = sba("hsb", [128, FC * TT], BF16)
    qsb, b_q = sba("qsb", [128, FC * TT], BF16)
    ksb, b_k = sba("ksb", [128, FC * TT], BF16)
    gsig, b_gsig = sba("gsig", [128, FC * TT], BF16)
    ktm, b_ktm = sba("ktm", [128, NTC * DM], BF16)
    vtm, b_vtm = sba("vtm", [128, NTC * DM], BF16)
    gks, b_gk = sba("gks", [128, NTC * DM], BF16)
    z1s, b_z1 = sba("z1s", [128, TT], BF16)
    eaq = [sba(f"eaq{i}", [128, TT], BF16) for i in range(2)]
    ean = [sba(f"ean{i}", [128, TT], BF16) for i in range(2)]
    eaend = [sba(f"eaend{i}", [128, NTC], F32) for i in range(2)]
    eras = [sba(f"era{i}", [128, TT], BF16) for i in range(2)]
    qts = [sba(f"qt{i}", [128, TT], BF16) for i in range(2)]
    kts = [sba(f"kt{i}", [128, TT], BF16) for i in range(2)]
    kbs = [sba(f"kb{i}", [128, TT], BF16) for i in range(2)]
    pms = [sba("pm0", [128, 1024], BF16)] * 2
    o2s = [sba("o20", [128, TT], BF16)] * 2
    rmsscr, b_rmsscr = sba("rmsscr", [128, TT], F32)
    rstdr = [sba("rstdr0", [128, TT], BF16)] * 2
    rgs = [sba("rg0", [128, TT], BF16)] * 2
    atis, b_ati = ksb, b_k
    S4 = [[sba(f"S{bb}_{i}", [128, HP * 64], BF16) for i in range(2)] for bb in range(BPC)]
    xres = [sba(f"xres{i}", [128, TT], F32) for i in range(2)]
    xmsub = [sba(f"xms{i}", [128, TT], F32) for i in range(2)]
    xb2, b_xb2 = xb, b_xb
    m1c, b_m1c = sba("m1c", [128, NTILES * FC], F32)

    dq_by_w = {}
    for tag, wh, wqh, rows, cols in (
            ("g", wgate_h, wgateq_h, DI, DM), ("u", wup_h, wupq_h, DI, DM),
            ("d", wdown_h, wdownq_h, DM, DI)):
        dq_by_w[tag] = [(wh, wqh, rc, pc, tag) for rc in range(rows // 128)
                        for pc in range(cols // 512)]
    dq_steps = dq_by_w["g"]

    def emit_dq(step, slot):
        wh, wqh, rc, pc, tag = step
        ti, bi = dq_in[slot % 2]
        to, bo = dq_o[slot % 2]
        em.do("sync", lambda e, t=ti, wh=wh, rc=rc, pc=pc: e.dma_start(
            out=t[:], in_=wh[rc * 128:(rc + 1) * 128, pc * 512:(pc + 1) * 512]),
            writes=(bi,), dma=f"d_dq{slot % 2}")
        em.do("vector", lambda e, t=ti: e.tensor_reduce(
            dq_s[:], t[:].rearrange("p (a b) -> p a b", b=32),
            axis=AX.X, op=ALU.max, apply_absolute_value=True),
            reads=(bi,), writes=(b_dqs,))
        em.do("vector", lambda e: e.tensor_scalar(
            dq_s[:], dq_s[:], 1.0 / 7.0, 1e-12, op0=ALU.mult, op1=ALU.max), rw=(b_dqs,))
        em.do("vector", lambda e: e.reciprocal(dq_r[:], dq_s[:]), reads=(b_dqs,), writes=(b_dqr,))
        em.do("vector", lambda e, t=ti: e.tensor_tensor(
            dq_wr[:].rearrange("p (a b) -> p a b", b=32),
            t[:].rearrange("p (a b) -> p a b", b=32),
            dq_r[:].rearrange("p (a b) -> p a b", b=1).broadcast_to([128, 16, 32]), op=ALU.mult),
            reads=(bi, b_dqr), writes=(b_dqwr,))
        em.do("vector", lambda e: e.tensor_scalar(
            dq_wr[:], dq_wr[:], C_RNE, -C_RNE, op0=ALU.add, op1=ALU.add), rw=(b_dqwr,))
        em.do("vector", lambda e, t=to: e.tensor_tensor(
            t[:].rearrange("p (a b) -> p a b", b=32),
            dq_wr[:].rearrange("p (a b) -> p a b", b=32),
            dq_s[:].rearrange("p (a b) -> p a b", b=1).broadcast_to([128, 16, 32]), op=ALU.mult),
            reads=(b_dqwr, b_dqs), writes=(bo,))
        em.do("scalar", lambda e, t=to, wqh=wqh, rc=rc, pc=pc: e.dma_start(
            out=wqh[rc * 128:(rc + 1) * 128, pc * 512:(pc + 1) * 512], in_=t[:]),
            reads=(bo,), dma=f"d_dqo_{tag}{slot % 2}")

    pp = [ps(f"pp{i}", [128, TT], scope=scopeB, init=snapA) for i in range(2)]
    pa, b_pa = ps("pa", [128, TT], scope=scopeB, init=snapA)
    pr, b_pr = ps("pr", [128, TT], scope=scopeB, init=snapA)
    pP, b_pP = ps("pP", [128, 1024], scope=scopeB, init=snapA)
    pO, b_pO = ps("pO", [128, TT], scope=scopeB, init=snapA)
    pkr, b_pkr = ps("pkr", [128, TT], scope=scopeB, init=snapA)

    em.do("sync", lambda e: e.dma_start(out=z1s[16:17, :], in_=onerow_h), rw=(b_z1,), dma="d_c3")
    em.freeze_wave((b_z1,), "d_c3")

    def ln_pass(xbt, b_xbt, dst, b_dst, cmod_ap, smod_ap, bb, track=None):
        p0t, b_p0 = pp[0]
        p1t, b_p1 = pp[1]
        for kc in range(FC):
            em.do("tensor", lambda e, kc=kc: e.matmul(
                p0t[:], ONES_LN, bslice(xbt, kc, TT), start=(kc == 0), stop=(kc == FC - 1)),
                reads=(b_xbt, b_ones), writes=(b_p0,) if kc == 0 else (), rw=(b_p0,) if kc else ())
        for kc in range(FC):
            t2, bx2 = x2s[kc % 2]
            em.do("scalar", lambda e, kc=kc, t2=t2: e.activation(
                t2[:], bslice(xbt, kc, TT), AF.Square, bias=ZERO, scale=1.0),
                reads=(b_xbt, b_kv), writes=(bx2,))
            em.do("tensor", lambda e, kc=kc, t2=t2: e.matmul(
                p1t[:], ONES_LN, t2[:], start=(kc == 0), stop=(kc == FC - 1)),
                reads=(bx2, b_ones), writes=(b_p1,) if kc == 0 else (), rw=(b_p1,) if kc else ())
        em.do("scalar", lambda e: e.copy(mu_bf[:], p0t[:]), reads=(b_p0,), writes=(b_mu,))
        em.do("scalar", lambda e: e.activation(mu2[:], mu_bf[:], AF.Square, bias=ZERO, scale=1.0),
              reads=(b_mu, b_kv), writes=(b_mu2,))
        em.do("vector", lambda e: e.tensor_tensor(varr[:], p1t[:], mu2[:], op=ALU.subtract),
              reads=(b_p1, b_mu2), writes=(b_varr,))
        em.do("scalar", lambda e: e.activation(rmsscr[:], varr[:], AF.Ln, bias=EPSC, scale=1.0),
              reads=(b_varr, b_kv), writes=(b_rmsscr,))
        em.do("scalar", lambda e: e.activation(rstd_bf[:], rmsscr[:], AF.Exp, bias=ZERO, scale=-0.5),
              reads=(b_rmsscr, b_kv), writes=(b_rstd,))
        em.do("vector", lambda e: e.tensor_tensor(
            xc1W[:].rearrange("p (a b) -> p a b", b=TT),
            xbt[:].rearrange("p (a b) -> p a b", b=TT),
            mu_bf[:].rearrange("p (o b) -> p o b", o=1).broadcast_to([128, FC, TT]),
            op=ALU.subtract), reads=(b_xbt, b_mu), writes=(b_xc1W,))
        em.do("vector", lambda e: e.tensor_tensor(
            xc1W[:].rearrange("p (a b) -> p a b", b=TT),
            xc1W[:].rearrange("p (a b) -> p a b", b=TT),
            rstd_bf[:].rearrange("p (o b) -> p o b", o=1).broadcast_to([128, FC, TT]),
            op=ALU.mult), reads=(b_rstd,), rw=(b_xc1W,))
        for fc_ in range(FC):
            em.do("vector", lambda e, fc_=fc_: e.tensor_scalar(
                bslice(dst, fc_, TT), bslice(xc1W, fc_, TT), cmod_ap(fc_, bb),
                smod_ap(fc_, bb), op0=ALU.mult, op1=ALU.add),
                reads=(b_xc1W, b_modc, b_modr), rw=(b_dst,))


    def proj_full(wsb, b_w, dst, b_dst, evac):
        for mc in range(FC):
            pt, b_pt = pp[mc % 2]
            for kc in range(FC):
                em.do("tensor", lambda e, pt=pt, mc=mc, kc=kc: e.matmul(
                    pt[:], wsb[:, kc * DM + mc * 128:kc * DM + (mc + 1) * 128],
                    bslice(hsb, kc, TT), start=(kc == 0), stop=(kc == FC - 1)),
                    reads=(b_w, b_h), writes=(b_pt,) if kc == 0 else (), rw=(b_pt,) if kc else ())
            evac(mc, pt, b_pt)

    def proj_tm(wsb, b_w, dst, b_dst):
        for tc in range(NTC):
            for half in range(2):
                pt, b_pt = pp[(tc * 2 + half) % 2]
                for kc in range(FC):
                    em.do("tensor", lambda e, pt=pt, tc=tc, half=half, kc=kc: e.matmul(
                        pt[:], hsb[:, kc * TT + tc * 128:kc * TT + (tc + 1) * 128],
                        wsb[:, kc * DM + half * 512:kc * DM + (half + 1) * 512],
                        start=(kc == 0), stop=(kc == FC - 1)),
                        reads=(b_w, b_h), writes=(b_pt,) if kc == 0 else (), rw=(b_pt,) if kc else ())
                em.do("scalar", lambda e, pt=pt, tc=tc, half=half: e.copy(
                    dst[:, tc * DM + half * 512:tc * DM + (half + 1) * 512], pt[:]),
                    reads=(b_pt,), rw=(b_dst,))

    tiles = [(bb, tt) for tt in range(TPS) for bb in range(BPC)]
    dq_per_tile = (len(dq_steps) + NTILES - 1) // NTILES

    def ln1_block(it2):
        bb2, tt2 = tiles[it2]
        tok2 = bb2 * L + tt2 * TT
        em.do("gpsimd", lambda e, tok2=tok2: e.dma_start(
            out=xb[:].rearrange("p (a t) -> p a t", a=FC),
            in_=x_h[:, :, tok2:tok2 + TT].rearrange("a p t -> p a t")),
            writes=(b_xb,), dma="d_xb")
        ln_pass(xb, b_xb, hsb, b_h, ca1_ap, sa_ap, bb2)

    ln1_block(0)
    for it, (bb, tt) in enumerate(tiles):
        tok0 = bb * L + tt * TT
        first_chunk_of_seq = (tt == 0)
        dq_lo = it * dq_per_tile
        dq_hi = min((it + 1) * dq_per_tile, len(dq_steps))
        dq_cursor = [dq_lo]

        def dq_drip(n=3):
            while dq_cursor[0] < dq_hi and n > 0:
                emit_dq(dq_steps[dq_cursor[0]], dq_cursor[0])
                dq_cursor[0] += 1
                n -= 1
        def ev_q(mc, pt, b_pt):
            em.do("vector", lambda e, mc=mc, pt=pt: e.tensor_scalar(
                bslice(qsb, mc, TT), pt[:], 0.0, None, op0=ALU.add),
                reads=(b_pt,), rw=(b_q,))
        def ev_k(mc, pt, b_pt):
            em.do("scalar", lambda e, mc=mc, pt=pt: e.copy(bslice(ksb, mc, TT), pt[:]),
                  reads=(b_pt,), rw=(b_k,))
        def ev_g(mc, pt, b_pt):
            em.do("scalar", lambda e, mc=mc, pt=pt: e.activation(
                bslice(gsig, mc, TT), pt[:], AF.Silu, bias=ZERO, scale=1.0),
                reads=(b_pt, b_kv), rw=(b_gsig,))
        proj_full(wqs, b_wq, qsb, b_q, ev_q)
        dq_drip(1)
        proj_full(wks, b_wk, ksb, b_k, ev_k)
        dq_drip(1)
        proj_tm(wks, b_wk, ktm, b_ktm)
        dq_drip(1)
        pzt, b_pzt = pp[0]
        for kc in range(FC):
            em.do("tensor", lambda e, kc=kc: e.matmul(
                pzt[0:16, :], wg1s[:, kc * 16:(kc + 1) * 16], bslice(hsb, kc, TT),
                start=(kc == 0), stop=(kc == FC - 1)),
                reads=(b_wg1, b_h), writes=(b_pzt,) if kc == 0 else (), rw=(b_pzt,) if kc else ())
        em.do("scalar", lambda e: e.copy(z1s[0:16, :], pzt[0:16, :]), reads=(b_pzt,), rw=(b_z1,))
        tpm0, bpm0 = pms[0]
        for tc in range(NTC):
            for half in range(2):
                em.do("tensor", lambda e, tc=tc, half=half: e.matmul(
                    pP[:, half * 512:(half + 1) * 512], z1s[0:17, tc * 128:(tc + 1) * 128],
                    wg2s[0:17, half * 512:(half + 1) * 512], start=True, stop=True),
                    reads=(b_z1, b_wg2), writes=(b_pP,) if half == 0 else (), rw=(b_pP,) if half else ())
            em.do("scalar", lambda e, tpm0=tpm0: e.activation(
                tpm0[:], pP[:], AF.Exp, bias=ZERO, scale=-1.0),
                reads=(b_pP, b_kv), writes=(bpm0,))
            em.do("scalar", lambda e, tc=tc, tpm0=tpm0: e.activation(
                gks[:, tc * DM:tc * DM + 1024], tpm0[:], AF.Ln, bias=ONEC, scale=1.0),
                reads=(bpm0, b_kv), rw=(b_gk,))
        proj_tm(wvs, b_wv, vtm, b_vtm)
        dq_drip(1)
        proj_full(wgs, b_wg, gsig, b_gsig, ev_g)
        dq_drip(2)
        if it + 1 < NTILES:
            ln1_block(it + 1)
        def scan_stage1(hp):
            sl = hp % 2
            te, be = eaq[sl]
            tn, bn = ean[sl]
            tee, bee = eaend[sl]
            tr, br = eras[sl]
            tqt, bqt = qts[sl]
            tkt, bkt = kts[sl]
            tkb, bkb = kbs[sl]
            for tc in range(NTC):
                em.do("tensor", lambda e, tc=tc, hp=hp: e.matmul(
                    pa[:, tc * 128:(tc + 1) * 128],
                    gks[:, tc * DM + hp * 128:tc * DM + (hp + 1) * 128], triA,
                    start=True, stop=True),
                    reads=(b_gk, b_tri), writes=(b_pa,) if tc == 0 else (), rw=(b_pa,) if tc else ())
            em.do("scalar", lambda e, te=te: e.activation(te[:], pa[:], AF.Exp, bias=LN8C, scale=1.0),
                  reads=(b_pa, b_kv), writes=(be,))
            em.do("scalar", lambda e, tn=tn: e.activation(tn[:], pa[:], AF.Exp, bias=ZERO, scale=-1.0),
                  reads=(b_pa, b_kv), writes=(bn,))
            em.do("scalar", lambda e, tee=tee: e.activation(
                tee[:], pa[:, 127:TT:128], AF.Exp, bias=ZERO, scale=1.0),
                reads=(b_pa, b_kv), writes=(bee,))
            for tc in range(NTC):
                em.do("tensor", lambda e, tc=tc, hp=hp: e.matmul(
                    pr[:, tc * 128:(tc + 1) * 128], triR,
                    gks[:, tc * DM + hp * 128:tc * DM + (hp + 1) * 128],
                    start=True, stop=True),
                    reads=(b_gk, b_tri), writes=(b_pr,) if tc == 0 else (), rw=(b_pr,) if tc else ())
            em.do("scalar", lambda e, tr=tr: e.activation(tr[:], pr[:], AF.Exp, bias=ZERO, scale=1.0),
                  reads=(b_pr, b_kv), writes=(br,))
            em.do("vector", lambda e, tqt=tqt, te=te, hp=hp: e.tensor_tensor(
                tqt[:], bslice(qsb, hp, TT), te[:], op=ALU.mult),
                reads=(b_q, be), writes=(bqt,))
            em.do("vector", lambda e, tkt=tkt, tn=tn, hp=hp: e.tensor_tensor(
                tkt[:], bslice(ksb, hp, TT), tn[:], op=ALU.mult),
                reads=(b_k, bn), writes=(bkt,))
            for tc in range(NTC):
                em.do("vector", lambda e, tkb=tkb, tr=tr, tc=tc, hp=hp: e.tensor_tensor(
                    tkb[:, tc * 128:(tc + 1) * 128],
                    ktm[:, tc * DM + hp * 128:tc * DM + (hp + 1) * 128],
                    tr[:, tc * 128:(tc + 1) * 128], op=ALU.mult),
                    reads=(b_ktm, br), writes=(bkb,) if tc == 0 else (), rw=(bkb,) if tc else ())

        def scan_stage2(hp):
            sl = hp % 2
            te, be = eaq[sl]
            tee, bee = eaend[sl]
            tqt, bqt = qts[sl]
            tkt, bkt = kts[sl]
            tkb, bkb = kbs[sl]
            tpm, bpm = pms[0]
            to2, bo2 = o2s[0]
            trs, brs = rstdr[0]
            trg, brg = rgs[0]
            for tc in range(NTC):
                em.do("tensor", lambda e, tkt=tkt, tqt=tqt, tc=tc: e.matmul(
                    pP[:, tc * 128:(tc + 1) * 128],
                    tkt[0:64, tc * 128:(tc + 1) * 128], tqt[0:64, tc * 128:(tc + 1) * 128],
                    start=True, stop=True),
                    reads=(bkt, bqt), writes=(b_pP,) if tc == 0 else (), rw=(b_pP,) if tc else ())
                em.do("tensor", lambda e, tkt=tkt, tqt=tqt, tc=tc: e.matmul(
                    pP[:, 512 + tc * 128:512 + (tc + 1) * 128],
                    tkt[64:128, tc * 128:(tc + 1) * 128], tqt[64:128, tc * 128:(tc + 1) * 128],
                    start=True, stop=True, tile_position=(64, 0)),
                    reads=(bkt, bqt), rw=(b_pP,))
            em.do("vector", lambda e, tpm=tpm: e.tensor_tensor(
                tpm[:], pP[:], mask8[:], op=ALU.mult),
                reads=(b_pP, b_mask8), writes=(bpm,))
            for tc in range(NTC):
                cur = tc % 2
                Sold, b_Sold = S4[bb][cur]
                Snew, b_Snew = S4[bb][1 - cur]
                glob_first = first_chunk_of_seq and tc == 0
                em.do("tensor", lambda e, tc=tc, hp=hp, tpm=tpm, gf=glob_first: e.matmul(
                    pO[0:64, tc * 128:(tc + 1) * 128],
                    vtm[:, tc * DM + hp * 128:tc * DM + hp * 128 + 64],
                    tpm[:, tc * 128:(tc + 1) * 128],
                    start=True, stop=gf),
                    reads=(b_vtm, bpm), writes=(b_pO,) if tc == 0 else (), rw=(b_pO,) if tc else ())
                if not glob_first:
                    em.do("tensor", lambda e, tc=tc, hp=hp, Sold=Sold, tqt=tqt: e.matmul(
                        pO[0:64, tc * 128:(tc + 1) * 128],
                        Sold[0:64, hp * 64:(hp + 1) * 64], tqt[0:64, tc * 128:(tc + 1) * 128],
                        start=False, stop=True),
                        reads=(b_Sold, bqt), rw=(b_pO,))
                em.do("tensor", lambda e, tc=tc, hp=hp, tpm=tpm, gf=glob_first: e.matmul(
                    pO[64:128, tc * 128:(tc + 1) * 128],
                    vtm[:, tc * DM + hp * 128 + 64:tc * DM + (hp + 1) * 128],
                    tpm[:, 512 + tc * 128:512 + (tc + 1) * 128],
                    start=True, stop=gf, tile_position=(0, 64)),
                    reads=(b_vtm, bpm), rw=(b_pO,))
                if not glob_first:
                    em.do("tensor", lambda e, tc=tc, hp=hp, Sold=Sold, tqt=tqt: e.matmul(
                        pO[64:128, tc * 128:(tc + 1) * 128],
                        Sold[64:128, hp * 64:(hp + 1) * 64], tqt[64:128, tc * 128:(tc + 1) * 128],
                        start=False, stop=True, tile_position=(64, 64)),
                        reads=(b_Sold, bqt), rw=(b_pO,))
                em.do("tensor", lambda e, tc=tc, hp=hp, tkb=tkb: e.matmul(
                    pkr[0:64, tc * 64:(tc + 1) * 64],
                    tkb[:, tc * 128:tc * 128 + 64],
                    vtm[:, tc * DM + hp * 128:tc * DM + hp * 128 + 64],
                    start=True, stop=True),
                    reads=(bkb, b_vtm), writes=(b_pkr,) if tc == 0 else (), rw=(b_pkr,) if tc else ())
                em.do("tensor", lambda e, tc=tc, hp=hp, tkb=tkb: e.matmul(
                    pkr[64:128, tc * 64:(tc + 1) * 64],
                    tkb[:, tc * 128 + 64:(tc + 1) * 128],
                    vtm[:, tc * DM + hp * 128 + 64:tc * DM + (hp + 1) * 128],
                    start=True, stop=True, tile_position=(0, 64)),
                    reads=(bkb, b_vtm), rw=(b_pkr,))
                if glob_first:
                    em.do("vector", lambda e, tc=tc, hp=hp, Snew=Snew: e.tensor_scalar(
                        Snew[:, hp * 64:(hp + 1) * 64], pkr[:, tc * 64:(tc + 1) * 64],
                        0.0, None, op0=ALU.add),
                        reads=(b_pkr,), rw=(b_Snew,))
                else:
                    em.do("vector", lambda e, tc=tc, hp=hp, Snew=Snew, Sold=Sold, tee=tee: e.scalar_tensor_tensor(
                        Snew[:, hp * 64:(hp + 1) * 64], Sold[:, hp * 64:(hp + 1) * 64],
                        tee[:, tc:tc + 1], pkr[:, tc * 64:(tc + 1) * 64],
                        op0=ALU.mult, op1=ALU.add),
                        reads=(b_Sold, bee, b_pkr), rw=(b_Snew,))
                em.do("scalar", lambda e, tc=tc, to2=to2: e.activation(
                    to2[:, tc * 128:(tc + 1) * 128], pO[:, tc * 128:(tc + 1) * 128],
                    AF.Square, bias=ZERO, scale=1.0),
                    reads=(b_pO, b_kv), writes=(bo2,) if tc == 0 else (), rw=(bo2,) if tc else ())
                em.do("tensor", lambda e, tc=tc, to2=to2: e.matmul(
                    pkr[:, 256 + (tc % 2) * 128:256 + (tc % 2 + 1) * 128],
                    BD_RMS, to2[:, tc * 128:(tc + 1) * 128],
                    start=True, stop=True),
                    reads=(bo2, b_ones), rw=(b_pkr,))
                if tc % 2 == 1:
                    em.do("scalar", lambda e, tc=tc: e.activation(
                        rmsscr[:, (tc - 1) * 128:(tc + 1) * 128], pkr[:, 256:512],
                        AF.Ln, bias=EPSC, scale=1.0),
                        reads=(b_pkr, b_kv), rw=(b_rmsscr,))
            em.do("scalar", lambda e, trs=trs: e.activation(trs[:], rmsscr[:], AF.Exp, bias=ZERO, scale=-0.5),
                  reads=(b_rmsscr, b_kv), writes=(brs,))
            em.do("vector", lambda e, trg=trg, trs=trs, hp=hp: e.tensor_tensor(
                trg[:], trs[:], bslice(gsig, hp, TT), op=ALU.mult),
                reads=(brs, b_gsig), writes=(brg,))
            em.do("vector", lambda e, trg=trg, hp=hp: e.scalar_tensor_tensor(
                bslice(atis, hp, TT), trg[:], gnw[:, 0:1], pO[:],
                op0=ALU.mult, op1=ALU.mult),
                reads=(brg, b_gnw, b_pO), rw=(b_ati,))

        scan_stage1(0)
        for hp in range(HP):
            if hp + 1 < HP:
                scan_stage1(hp + 1)
            scan_stage2(hp)
        dq_drip(99)
        for mc in range(FC):
            xr, b_xr = xres[mc % 2]
            em.do("sync", lambda e, mc=mc, tok0=tok0, xr=xr: e.dma_start(
                out=xr[:], in_=x_h[mc, :, tok0:tok0 + TT]), writes=(b_xr,), dma=f"d_xr{mc % 2}")
            pt, b_pt = pp[mc % 2]
            for kc in range(FC):
                em.do("tensor", lambda e, pt=pt, mc=mc, kc=kc: e.matmul(
                    pt[:], wos[:, kc * DM + mc * 128:kc * DM + (mc + 1) * 128],
                    bslice(atis, kc, TT), start=(kc == 0), stop=(kc == FC - 1)),
                    reads=(b_wo, b_ati), writes=(b_pt,) if kc == 0 else (), rw=(b_pt,) if kc else ())
            xm, b_xm = xmsub[mc % 2]
            em.do("vector", lambda e, pt=pt, xr=xr, xm=xm, mc=mc, bb=bb: e.scalar_tensor_tensor(
                xm[:], pt[:], ga_ap(mc, bb), xr[:], op0=ALU.mult, op1=ALU.add),
                reads=(b_pt, b_modg, b_xr), writes=(b_xm,))
            em.do("sync", lambda e, xm=xm, mc=mc, tok0=tok0: e.dma_start(
                out=xmid_h[mc, :, tok0:tok0 + TT], in_=xm[:]), reads=(b_xm,), dma=f"d_xm{mc % 2}")
            em.do("scalar", lambda e, xm=xm, mc=mc: e.copy(bslice(xb2, mc, TT), xm[:]),
                  reads=(b_xm,), rw=(b_xb2,))
        p0t, b_p0 = pp[0]
        p1t, b_p1 = pp[1]
        for kc in range(FC):
            em.do("tensor", lambda e, kc=kc: e.matmul(
                p0t[:], ONES_LN, bslice(xb2, kc, TT), start=(kc == 0), stop=(kc == FC - 1)),
                reads=(b_xb2, b_ones), writes=(b_p0,) if kc == 0 else (), rw=(b_p0,) if kc else ())
        for kc in range(FC):
            t2, bx2 = x2s[kc % 2]
            em.do("scalar", lambda e, kc=kc, t2=t2: e.activation(
                t2[:], bslice(xb2, kc, TT), AF.Square, bias=ZERO, scale=1.0),
                reads=(b_xb2, b_kv), writes=(bx2,))
            em.do("tensor", lambda e, kc=kc, t2=t2: e.matmul(
                p1t[:], ONES_LN, t2[:], start=(kc == 0), stop=(kc == FC - 1)),
                reads=(bx2, b_ones), writes=(b_p1,) if kc == 0 else (), rw=(b_p1,) if kc else ())
        em.do("scalar", lambda e: e.copy(mu_bf[:], p0t[:]), reads=(b_p0,), writes=(b_mu,))
        em.do("scalar", lambda e: e.activation(mu2[:], mu_bf[:], AF.Square, bias=ZERO, scale=1.0),
              reads=(b_mu, b_kv), writes=(b_mu2,))
        em.do("vector", lambda e: e.tensor_tensor(varr[:], p1t[:], mu2[:], op=ALU.subtract),
              reads=(b_p1, b_mu2), writes=(b_varr,))
        em.do("scalar", lambda e: e.activation(rmsscr[:], varr[:], AF.Ln, bias=EPSC, scale=1.0),
              reads=(b_varr, b_kv), writes=(b_rmsscr,))
        em.do("scalar", lambda e: e.activation(rstd_bf[:], rmsscr[:], AF.Exp, bias=ZERO, scale=-0.5),
              reads=(b_rmsscr, b_kv), writes=(b_rstd,))
        em.do("vector", lambda e: e.tensor_tensor(
            xc1W[:].rearrange("p (a b) -> p a b", b=TT),
            xb2[:].rearrange("p (a b) -> p a b", b=TT),
            mu_bf[:].rearrange("p (o b) -> p o b", o=1).broadcast_to([128, FC, TT]),
            op=ALU.subtract), reads=(b_xb2, b_mu), writes=(b_xc1W,))
        em.do("vector", lambda e: e.tensor_tensor(
            xc1W[:].rearrange("p (a b) -> p a b", b=TT),
            xc1W[:].rearrange("p (a b) -> p a b", b=TT),
            rstd_bf[:].rearrange("p (o b) -> p o b", o=1).broadcast_to([128, FC, TT]),
            op=ALU.mult), reads=(b_rstd,), rw=(b_xc1W,))
        for fc_ in range(FC):
            em.do("vector", lambda e, fc_=fc_, bb=bb: e.tensor_scalar(
                bslice(xc1W, fc_, TT), bslice(xc1W, fc_, TT), cb1_ap(fc_, bb),
                sb_ap(fc_, bb), op0=ALU.mult, op1=ALU.add),
                reads=(b_modc, b_modr), rw=(b_xc1W,))
        em.do("vector", lambda e, it=it: e.tensor_reduce(
            m1c[:, it * FC:(it + 1) * FC],
            xc1W[:].rearrange("p (a b) -> p a b", b=TT), axis=AX.X, op=ALU.max,
            apply_absolute_value=True), reads=(b_xc1W,), rw=(b_m1c,))
        em.do("gpsimd", lambda e, tok0=tok0: e.dma_start(
            out=h2_h[:, :, tok0:tok0 + TT].rearrange("a p b -> p a b"),
            in_=xc1W[:].rearrange("p (a b) -> p a b", b=TT)),
            reads=(b_xc1W,), dma=f"d_h2{it % 2}")

    b_xmid_dram = Buf("xmid_d")
    b_h2_dram = Buf("h2_d")
    for i in range(2):
        b_xmid_dram.wt[f"d_xm{i}"] = em.counts.get(f"d_xm{i}", 0)
    for i in range(2):
        b_h2_dram.wt[f"d_h2{i}"] = em.counts.get(f"d_h2{i}", 0)
    b_wgq_dram = Buf("wgq_d")
    b_wgq_dram.wt = {"d_dqo_g0": em.counts.get("d_dqo_g0", 0),
                     "d_dqo_g1": em.counts.get("d_dqo_g1", 0)}

    if "attn" in taps:
        txm = nc.dram_tensor("tap_xmid", [FC, 128, BPC * L], F32, kind="ExternalOutput").ap()
        th2 = nc.dram_tensor("tap_h2", [FC, 128, BPC * L], BF16, kind="ExternalOutput").ap()
        t1 = em.do("sync", lambda e: e.dma_start(out=txm, in_=xmid_h), reads=(b_xmid_dram,), dma="d_tap")
        t2 = em.do("sync", lambda e: e.dma_start(out=th2, in_=h2_h), reads=(b_h2_dram,), dma="d_tap")
        em._push("sync", None, [t1, t2], None, 0)

    if upto == "attn":
        with nc.Block() as block:
            sems = {s: ec(nc.semaphore(s)) for s in em.counts}
            em.replay(block, sems)
        scopeB.close()
        st.close()
        return nc

    SC_R1, SC_S1, SC_R3, SC_S3, SC_R2, SC_S2, SC_K1, SC_K3, SC_S5 = range(9)

    def scol(c):
        return scl[:, c:c + 1]

    def cc_round(src_col_fns, out_fns):
        for col, fn in src_col_fns:
            fn()
        em.do("gpsimd", lambda e: e.dma_start(out=cc_in_h, in_=ccs[:]),
              reads=(b_ccs,), dma="d_cci")
        b_ccin = Buf("ccin")
        b_ccin.wt = {"d_cci": em.counts["d_cci"]}
        em.do("gpsimd", lambda e: e.collective_compute(
            "AllReduce", ALU.max, replica_groups=[list(range(n_cores))],
            ins=[cc_in_h], outs=[cc_out_h]),
            reads=(b_ccin,), writes=(b_ccout,), cc=True)
        em.do("gpsimd", lambda e: e.dma_start(out=ccr[:], in_=cc_out_h),
              reads=(b_ccout,), writes=(b_ccr,), dma="d_ccr")
        for fn in out_fns:
            fn()

    b_ccout = Buf("ccout")

    def finalize_max(mx, b_mx, ncols, col, psum_pair):
        pt, b_pt = psum_pair
        em.do("vector", lambda e: e.tensor_reduce(
            tiny[:, 0:1], mx[:, 0:ncols], axis=AX.X, op=ALU.max),
            reads=(b_mx,), writes=(b_tiny,))
        em.do("tensor", lambda e, pt=pt: e.transpose(pt[0:1, 0:128], tiny[:, 0:1], idn[:]),
              reads=(b_tiny, b_idn), writes=(b_pt,))
        em.do("vector", lambda e, col=col, pt=pt: e.tensor_reduce(
            ccs[0:1, col:col + 1], pt[0:1, 0:128], axis=AX.X, op=ALU.max),
            reads=(b_pt,), rw=(b_ccs,))

    def bcast_scales(psum_pair):
        pt, b_pt = psum_pair
        em.do("tensor", lambda e, pt=pt: e.matmul(
            pt[:, 0:12], bone[:], scrow[0:1, 0:12], start=True, stop=True),
            reads=(b_bone, b_scrow), writes=(b_pt,))
        em.do("vector", lambda e, pt=pt: e.tensor_scalar(
            scl[:], pt[:, 0:12], 1.0, None, op0=ALU.mult),
            reads=(b_pt,), writes=(b_scl,))

    def scale_pair(src_ap, b_src, rcol, scol_, psum_pair):
        def fns():
            em.do("vector", lambda e, c=scol_: e.tensor_scalar(
                scrow[0:1, c:c + 1], src_ap, 1.0 / 32767.0, 1e-12,
                op0=ALU.mult, op1=ALU.max), reads=(b_src,), rw=(b_scrow,))
            em.do("vector", lambda e, c=rcol, cs=scol_: e.reciprocal(
                scrow[0:1, c:c + 1], scrow[0:1, cs:cs + 1]), rw=(b_scrow,))
            bcast_scales(psum_pair)
        return fns

    em.do("vector", lambda e: e.memset(ccs[:], 0.0), writes=(b_ccs,))
    em.do("vector", lambda e: e.memset(scl[:], 0.0), writes=(b_scl,))
    em.do("vector", lambda e: e.memset(scrow[:], 0.0), writes=(b_scrow,))
    em.do("vector", lambda e: e.memset(tiny[:], 0.0), writes=(b_tiny,))

    finalize_max(m1c, b_m1c, NTILES * FC, 0, pp[1])
    cc_round([], [scale_pair(ccr[0:1, 0:1], b_ccr, SC_R1, SC_S1, pp[0])])

    if upto == "attn2":
        with nc.Block() as block:
            sems = {s: ec(nc.semaphore(s)) for s in em.counts}
            em.replay(block, sems)
        scopeB.close()
        st.close()
        return nc

    scopeB.close()
    snapB = em.snapshot()
    scopeC = ExitStack()

    def sbc(name, shape, dt):
        return sb(name, shape, dt, scope=scopeC, init=snapB)

    gmx, b_gmx = sbc("gmx", [128, NTILES * MI], BF16)
    umx, b_umx = sbc("umx", [128, NTILES * MI], BF16)
    tmx, b_tmx = sbc("tmx", [128, NTILES * MI], BF16)
    pmx, b_pmx = sbc("pmx", [128, NTILES * MI], BF16)
    pf = [ps(f"pf{i}", [128, TT], scope=scopeC, init=snapB) for i in range(8)]

    scopeB2 = ExitStack()

    def sbb(name, shape, dt):
        return sb(name, shape, dt, scope=scopeB2, init=snapB)

    fscr = [sbb(f"fscr{i}", [128, 2 * TT], F32) for i in range(2)]
    gins = [sbb(f"gin{i}", [128, 2 * TT], BF16) for i in range(2)]
    iscr = [sbb(f"iscr{i}", [128, 2 * TT], I16) for i in range(2)]
    uscr = [sbb(f"uscr{i}", [128, 2 * TT], U16) for i in range(2)]
    f3scr = [sbb(f"f3scr{i}", [128, 2 * TT], F32) for i in range(2)]
    p2c = [sbb(f"p2c{i}", [128, 2 * TT], F32) for i in range(2)]
    tbs = [sbb(f"tb{i}", [128, 2 * TT], BF16) for i in range(2)]
    wall = [sbb(f"wall{i}", [128, NTILES * TT], BF16) for i in range(2)]

    scopeC1 = ExitStack()
    xq, b_xq = sb("xq", [128, FC * BPC * L], BF16, scope=scopeC1, init=snapB)
    wgT, b_wgT = sb("wgT", [128, FC * DI], BF16, scope=scopeC1, init=snapB)
    wupT, b_wupT = wgT, b_wgT
    h2in = [sb(f"h2in{i}", [128, BPC * L // 4], BF16, scope=scopeC1, init=snapB) for i in range(2)]
    _wave_deps = b_wgq_dram.deps_read() + b_wgT.deps_write()
    for kc in range(FC):
        em.do("sync", lambda e, kc=kc: e.dma_start_transpose(
            out=wgT[:, kc * DI:(kc + 1) * DI], in_=wgateq_h[:, kc * 128:(kc + 1) * 128]),
            deps=_wave_deps, dma="d_wgt")
    em.freeze_wave((b_wgT,), "d_wgt")

    for fc_ in range(FC):
        for hh in range(4):
            sl2 = (fc_ * 4 + hh) % 2
            hi, b_hi = h2in[sl2]
            em.do("sync", lambda e, fc_=fc_, hh=hh, hi=hi: e.dma_start(
                out=hi[:], in_=h2_h[fc_, :, hh * 1024:(hh + 1) * 1024]),
                deps=b_h2_dram.deps_read(), writes=(b_hi,), dma=f"d_h2in{sl2}")
            fs, b_fs = fscr[hh % 2]
            em.do("vector", lambda e, hi=hi, fs=fs: e.tensor_scalar(
                fs[:, 0:1024], hi[:], scol(SC_R1), C_RNE,
                op0=ALU.mult, op1=ALU.add),
                reads=(b_hi, b_scl), writes=(b_fs,))
            em.do("vector", lambda e, fc_=fc_, hh=hh, fs=fs: e.tensor_scalar(
                xq[:, fc_ * BPC * L + hh * 1024:fc_ * BPC * L + (hh + 1) * 1024],
                fs[:, 0:1024], -C_RNE, scol(SC_S1), op0=ALU.add, op1=ALU.mult),
                reads=(b_fs, b_scl), rw=(b_xq,))

    def mm_block(wT, b_wT, tt, m, pt, b_pt):
        for kc in range(FC):
            em.do("tensor", lambda e, kc=kc, tt=tt, m=m, pt=pt, wT=wT: e.matmul(
                pt[:], wT[:, kc * DI + m * 128:kc * DI + (m + 1) * 128],
                xq[:, kc * BPC * L + tt * TT:kc * BPC * L + (tt + 1) * TT],
                start=(kc == 0), stop=(kc == FC - 1)),
                reads=(b_wT, b_xq), writes=(b_pt,) if kc == 0 else (), rw=(b_pt,) if kc else ())

    b_g_dram = Buf("g_d")
    b_u_dram = Buf("u_d")
    b_t_dram = Buf("t_d")
    dq_u = dq_by_w["u"]
    for m in range(MI):
        wa, b_wa = wall[m % 2]
        for kc in range(FC):
            for tt in range(NTILES):
                pt, b_pt = pf[tt]
                em.do("tensor", lambda e, kc=kc, tt=tt, m=m, pt=pt: e.matmul(
                    pt[:], wgT[:, kc * DI + m * 128:kc * DI + (m + 1) * 128],
                    xq[:, kc * BPC * L + tt * TT:kc * BPC * L + (tt + 1) * TT],
                    start=(kc == 0), stop=(kc == FC - 1)),
                    reads=(b_wgT, b_xq), writes=(b_pt,) if kc == 0 else (), rw=(b_pt,) if kc else ())
        for tt in range(NTILES):
            pt, b_pt = pf[tt]
            em.do("scalar", lambda e, pt=pt, wa=wa, tt=tt: e.copy(
                wa[:, tt * TT:(tt + 1) * TT], pt[:]), reads=(b_pt,), rw=(b_wa,))
        em.do("vector", lambda e, wa=wa, m=m: e.tensor_reduce(
            gmx[:, m * NTILES:(m + 1) * NTILES],
            wa[:].rearrange("p (a b) -> p a b", b=TT), axis=AX.X, op=ALU.max,
            apply_absolute_value=True), reads=(b_wa,), rw=(b_gmx,))
        em.do("gpsimd", lambda e, wa=wa, m=m: e.dma_start(
            out=g_h[:, m].rearrange("a p b -> p a b"),
            in_=wa[:].rearrange("p (a b) -> p a b", b=TT)),
            reads=(b_wa,), rw=(b_g_dram,), dma=f"d_wsg{m % 2}")
        if m * 2 < len(dq_u):
            emit_dq(dq_u[m * 2], m * 2)
            if m * 2 + 1 < len(dq_u):
                emit_dq(dq_u[m * 2 + 1], m * 2 + 1)
    b_wuq_dram = Buf("wuq_d")
    b_wuq_dram.wt = {"d_dqo_u0": em.counts.get("d_dqo_u0", 0),
                     "d_dqo_u1": em.counts.get("d_dqo_u1", 0)}
    _wave_depsu = b_wuq_dram.deps_read() + b_wgT.deps_write()
    for kc in range(FC):
        em.do("sync", lambda e, kc=kc: e.dma_start_transpose(
            out=wupT[:, kc * DI:(kc + 1) * DI], in_=wupq_h[:, kc * 128:(kc + 1) * 128]),
            deps=_wave_depsu, dma="d_wut")
    em.freeze_wave((b_wupT,), "d_wut")
    finalize_max(gmx, b_gmx, NTILES * MI, 1, pf[4])

    dq_d = dq_by_w["d"]

    def dq_d_range(lo, hi):
        def f():
            for s_ in range(lo, hi):
                emit_dq(dq_d[s_], s_)
        return f

    cc_round([], [dq_d_range(0, 24),
                  scale_pair(ccr[0:1, 1:2], b_ccr, SC_R3, SC_S3, pf[5])])

    for m in range(MI):
        wa, b_wa = wall[m % 2]
        for kc in range(FC):
            for tt in range(NTILES):
                pt, b_pt = pf[tt]
                em.do("tensor", lambda e, kc=kc, tt=tt, m=m, pt=pt: e.matmul(
                    pt[:], wupT[:, kc * DI + m * 128:kc * DI + (m + 1) * 128],
                    xq[:, kc * BPC * L + tt * TT:kc * BPC * L + (tt + 1) * TT],
                    start=(kc == 0), stop=(kc == FC - 1)),
                    reads=(b_wupT, b_xq), writes=(b_pt,) if kc == 0 else (), rw=(b_pt,) if kc else ())
        for tt in range(NTILES):
            pt, b_pt = pf[tt]
            if tt % 2 == 0:
                em.do("scalar", lambda e, pt=pt, wa=wa, tt=tt: e.copy(
                    wa[:, tt * TT:(tt + 1) * TT], pt[:]), reads=(b_pt,), rw=(b_wa,))
            else:
                em.do("vector", lambda e, pt=pt, wa=wa, tt=tt: e.tensor_scalar(
                    wa[:, tt * TT:(tt + 1) * TT], pt[:], 0.0, None, op0=ALU.add),
                    reads=(b_pt,), rw=(b_wa,))
        em.do("vector", lambda e, wa=wa, m=m: e.tensor_reduce(
            umx[:, m * NTILES:(m + 1) * NTILES],
            wa[:].rearrange("p (a b) -> p a b", b=TT), axis=AX.X, op=ALU.max,
            apply_absolute_value=True), reads=(b_wa,), rw=(b_umx,))
        em.do("gpsimd", lambda e, wa=wa, m=m: e.dma_start(
            out=u_h[:, m].rearrange("a p b -> p a b"),
            in_=wa[:].rearrange("p (a b) -> p a b", b=TT)),
            reads=(b_wa,), rw=(b_u_dram,), dma=f"d_wsu{m % 2}")
        for bi in range(4):
            tt0 = bi * 2
            sl = (m * 4 + bi) % 2
            gi, b_gi = gins[sl]
            em.do("sync", lambda e, gi=gi, tt0=tt0, m=m: e.dma_start(
                out=gi[:].rearrange("p (a b) -> p a b", b=TT),
                in_=g_h[tt0:tt0 + 2, m].rearrange("a p b -> p a b")),
                deps=b_g_dram.deps_read(), writes=(b_gi,), dma=f"d_gin{sl}")
            s1_, b_s1 = fscr[sl]
            em.do("scalar", lambda e, gi=gi, s1_=s1_: e.activation(
                s1_[:], gi[:], AF.Copy, bias=C_RNE, scale=scol(SC_R3)),
                reads=(b_gi, b_scl), writes=(b_s1,))
            gint, b_gint = iscr[sl]
            em.do("scalar", lambda e, s1_=s1_, gint=gint: e.activation(
                gint[:], s1_[:], AF.Copy, bias=-C_RNE, scale=1.0),
                reads=(b_s1,), writes=(b_gint,))
            sg_, b_sg = f3scr[sl]
            em.do("scalar", lambda e, gint=gint, sg_=sg_: e.activation(
                sg_[:], gint[:], AF.Sigmoid, bias=ZERO, scale=scol(SC_S3)),
                reads=(b_gint, b_scl, b_kv), writes=(b_sg,))
            pc_, b_pc = p2c[sl]
            em.do("scalar", lambda e, sg_=sg_, pc_=pc_: e.activation(
                pc_[:], sg_[:], AF.Copy, bias=C_RNE, scale=65536.0),
                reads=(b_sg,), writes=(b_pc,))
            su, b_su = uscr[sl]
            em.do("vector", lambda e, pc_=pc_, su=su: e.tensor_scalar(
                su[:], pc_[:], -C_RNE, 65535.0, op0=ALU.add, op1=ALU.min),
                reads=(b_pc,), writes=(b_su,))
            tb, b_tb = tbs[sl]
            em.do("vector", lambda e, gint=gint, su=su, tb=tb: e.tensor_tensor(
                tb[:], gint[:], su[:], op=ALU.mult), reads=(b_gint, b_su), writes=(b_tb,))
            em.do("vector", lambda e, tb=tb, m=m, tt0=tt0: e.tensor_reduce(
                tmx[:, m * NTILES + tt0:m * NTILES + tt0 + 2],
                tb[:].rearrange("p (a b) -> p a b", b=TT), axis=AX.X, op=ALU.max,
                apply_absolute_value=True), reads=(b_tb,), rw=(b_tmx,))
            em.do("gpsimd", lambda e, tb=tb, tt0=tt0, m=m: e.dma_start(
                out=t_h[tt0:tt0 + 2, m].rearrange("a p b -> p a b"),
                in_=tb[:].rearrange("p (a b) -> p a b", b=TT)),
                reads=(b_tb,), rw=(b_t_dram,), dma=f"d_wst{sl}")
    finalize_max(umx, b_umx, NTILES * MI, 2, pf[4])
    finalize_max(tmx, b_tmx, NTILES * MI, 3, pf[5])
    scopeC1.close()
    scopeB2.close()
    snapB2 = em.snapshot()
    tiW = [sb(f"tiW{i}", [128, 4 * TT], BF16, scope=scopeC, init=snapB2) for i in range(2)]
    uiW = [sb(f"uiW{i}", [128, 4 * TT], BF16, scope=scopeC, init=snapB2) for i in range(2)]
    s1W = [sb(f"s1W{i}", [128, 4 * TT], F32, scope=scopeC, init=snapB2) for i in range(2)]
    tintW = [sb(f"tintW{i}", [128, 4 * TT], I16, scope=scopeC, init=snapB2) for i in range(2)]
    s2W = [sb(f"s2W{i}", [128, 4 * TT], F32, scope=scopeC, init=snapB2) for i in range(2)]
    uintW = [sb(f"uintW{i}", [128, 4 * TT], I16, scope=scopeC, init=snapB2) for i in range(2)]
    pbW = s2W
    s3W = s1W

    SC_S4 = 12
    SC_R4 = 13
    SC_S42 = 14

    def post_ar3():
        em.do("vector", lambda e: e.tensor_scalar(
            scrow[0:1, SC_S2:SC_S2 + 1], ccr[0:1, 2:3], 1.0 / 32767.0, 1e-12,
            op0=ALU.mult, op1=ALU.max), reads=(b_ccr,), rw=(b_scrow,))
        em.do("vector", lambda e: e.reciprocal(
            scrow[0:1, SC_R2:SC_R2 + 1], scrow[0:1, SC_S2:SC_S2 + 1]), rw=(b_scrow,))
        em.do("vector", lambda e: e.tensor_scalar(
            scrow[0:1, SC_S4:SC_S4 + 1], ccr[0:1, 3:4], scrow[0:1, SC_S3:SC_S3 + 1],
            2.0 ** -16, op0=ALU.mult, op1=ALU.mult), reads=(b_ccr,), rw=(b_scrow,))
        em.do("vector", lambda e: e.tensor_scalar(
            scrow[0:1, SC_S4:SC_S4 + 1], scrow[0:1, SC_S4:SC_S4 + 1], 1.0 / 32767.0,
            1e-12, op0=ALU.mult, op1=ALU.max), rw=(b_scrow,))
        em.do("vector", lambda e: e.reciprocal(
            scrow[0:1, SC_R4:SC_R4 + 1], scrow[0:1, SC_S4:SC_S4 + 1]), rw=(b_scrow,))
        em.do("vector", lambda e: e.tensor_scalar(
            scrow[0:1, SC_K1:SC_K1 + 1], scrow[0:1, SC_R4:SC_R4 + 1],
            scrow[0:1, SC_S3:SC_S3 + 1], 2.0 ** -16, op0=ALU.mult, op1=ALU.mult),
            rw=(b_scrow,))
        em.do("vector", lambda e: e.tensor_scalar(
            scrow[0:1, SC_S42:SC_S42 + 1], scrow[0:1, SC_S4:SC_S4 + 1],
            scrow[0:1, SC_S2:SC_S2 + 1], None, op0=ALU.mult), rw=(b_scrow,))
        bcast_scales(pf[5])

    cc_round([], [dq_d_range(24, 48), post_ar3])

    def p3_chain(tt, mb, and_pq=None):
        m0 = mb * 4
        sl = (tt * 6 + mb) % 2
        qsl = (tt * 6 + mb) % 3
        ti_, b_ti = tiW[sl]
        ui_, b_ui = uiW[sl]
        em.do("gpsimd", lambda e, ti_=ti_, tt=tt, m0=m0: e.dma_start(
            out=ti_[:].rearrange("p (a b) -> p a b", b=TT),
            in_=t_h[tt, m0:m0 + 4].rearrange("a p b -> p a b")),
            deps=b_t_dram.deps_read(), writes=(b_ti,), dma=f"d_gti{qsl}")
        em.do("gpsimd", lambda e, ui_=ui_, tt=tt, m0=m0: e.dma_start(
            out=ui_[:].rearrange("p (a b) -> p a b", b=TT),
            in_=u_h[tt, m0:m0 + 4].rearrange("a p b -> p a b")),
            deps=b_u_dram.deps_read(), writes=(b_ui,), dma=f"d_gui{qsl}")
        s1_, b_s1 = s1W[sl]
        em.do("scalar", lambda e, ti_=ti_, s1_=s1_: e.activation(
            s1_[:], ti_[:], AF.Copy, bias=C_RNE, scale=scol(SC_K1)),
            reads=(b_ti, b_scl), writes=(b_s1,))
        tint, b_tint = tintW[sl]
        em.do("scalar", lambda e, s1_=s1_, tint=tint: e.activation(
            tint[:], s1_[:], AF.Copy, bias=-C_RNE, scale=1.0),
            reads=(b_s1,), writes=(b_tint,))
        s2_, b_s2 = s2W[sl]
        em.do("scalar", lambda e, ui_=ui_, s2_=s2_: e.activation(
            s2_[:], ui_[:], AF.Copy, bias=C_RNE, scale=scl[:, SC_R2:SC_R2 + 1]),
            reads=(b_ui, b_scl), writes=(b_s2,))
        uint, b_uint = uintW[sl]
        em.do("vector", lambda e, s2_=s2_, uint=uint: e.tensor_scalar(
            uint[:], s2_[:], -C_RNE, None, op0=ALU.add), reads=(b_s2,), writes=(b_uint,))
        pb, b_pb = pbW[sl]
        em.do("vector", lambda e, tint=tint, uint=uint, pb=pb: e.tensor_tensor(
            pb[:], tint[:], uint[:], op=ALU.mult), reads=(b_tint, b_uint), writes=(b_pb,))
        if and_pq is None:
            idx0 = tt * MI + m0
            em.do("vector", lambda e, pb=pb, idx0=idx0: e.tensor_reduce(
                pmx[:, idx0:idx0 + 4], pb[:].rearrange("p (a b) -> p a b", b=TT),
                axis=AX.X, op=ALU.max, apply_absolute_value=True),
                reads=(b_pb,), rw=(b_pmx,))
        else:
            pq, b_pq = and_pq
            s3_, b_s3 = s3W[sl]
            em.do("vector", lambda e, pb=pb, s3_=s3_: e.tensor_scalar(
                s3_[:], pb[:], scol(SC_K3), C_RNE, op0=ALU.mult, op1=ALU.add),
                reads=(b_pb, b_scl), writes=(b_s3,))
            em.do("vector", lambda e, s3_=s3_, m0=m0, pq=pq: e.tensor_scalar(
                pq[:, m0 * TT:(m0 + 4) * TT], s3_[:], -C_RNE, scol(SC_S5),
                op0=ALU.add, op1=ALU.mult),
                reads=(b_s3, b_scl), rw=(b_pq,))

    for tt in range(NTILES):
        for mb in range(6):
            p3_chain(tt, mb)
    finalize_max(pmx, b_pmx, NTILES * MI, 0, pf[4])

    def post_ar4():
        em.do("vector", lambda e: e.tensor_scalar(
            scrow[0:1, SC_S5:SC_S5 + 1], ccr[0:1, 0:1], scrow[0:1, SC_S42:SC_S42 + 1],
            None, op0=ALU.mult), reads=(b_ccr,), rw=(b_scrow,))
        em.do("vector", lambda e: e.tensor_scalar(
            scrow[0:1, SC_S5:SC_S5 + 1], scrow[0:1, SC_S5:SC_S5 + 1], 1.0 / 32767.0,
            1e-12, op0=ALU.mult, op1=ALU.max), rw=(b_scrow,))
        em.do("vector", lambda e: e.reciprocal(
            scrow[0:1, 15:16], scrow[0:1, SC_S5:SC_S5 + 1]), rw=(b_scrow,))
        em.do("vector", lambda e: e.tensor_scalar(
            scrow[0:1, SC_K3:SC_K3 + 1], scrow[0:1, 15:16],
            scrow[0:1, SC_S42:SC_S42 + 1], None, op0=ALU.mult), rw=(b_scrow,))
        bcast_scales(pf[5])

    cc_round([], [post_ar4])

    snapC = em.snapshot()
    pqs = [sb(f"pq{i}", [128, MI * TT], BF16, scope=scopeC, init=snapC) for i in range(2)]
    wdT, b_wdT = sb("wdT", [128, MI * 1024], BF16, scope=scopeC, init=snapC)
    xmin = [sb(f"xmin{i}", [128, TT], F32, scope=scopeC, init=snapC) for i in range(2)]
    otile = [sb(f"ot{i}", [128, TT], F32, scope=scopeC, init=snapC) for i in range(2)]
    b_wdq_dram = Buf("wdq_d")
    b_wdq_dram.wt = {"d_dqo_d0": em.counts.get("d_dqo_d0", 0),
                     "d_dqo_d1": em.counts.get("d_dqo_d1", 0)}
    _wave_deps2 = b_wdq_dram.deps_read() + b_wdT.deps_write()
    for kcd in range(MI):
        em.do("sync", lambda e, kcd=kcd: e.dma_start_transpose(
            out=wdT[:, kcd * 1024:(kcd + 1) * 1024], in_=wdownq_h[:, kcd * 128:(kcd + 1) * 128]),
            deps=_wave_deps2, dma="d_wdt")
    em.freeze_wave((b_wdT,), "d_wdt")

    out_toks = []

    for mb in range(6):
        p3_chain(0, mb, and_pq=pqs[0])
    for tt in range(NTILES):
        bb = tt // TPS
        tok0 = tt * TT
        pq, b_pq = pqs[tt % 2]
        if tt + 1 < NTILES:
            for mb in range(6):
                p3_chain(tt + 1, mb, and_pq=pqs[(tt + 1) % 2])
        for mc in range(FC):
            if mc == 0:
                xi2, b_xi2 = xmin[0]
                em.do("sync", lambda e, xi2=xi2, tok0=tok0: e.dma_start(
                    out=xi2[:], in_=xmid_h[0, :, tok0:tok0 + TT]),
                    deps=b_xmid_dram.deps_read(), writes=(b_xi2,), dma="d_xmin0")
            pt, b_pt = pf[mc % 6]
            for kcd in range(MI):
                em.do("tensor", lambda e, pt=pt, mc=mc, kcd=kcd, pq=pq: e.matmul(
                    pt[:], wdT[:, kcd * 1024 + mc * 128:kcd * 1024 + (mc + 1) * 128],
                    pq[:, kcd * TT:(kcd + 1) * TT],
                    start=(kcd == 0), stop=(kcd == MI - 1)),
                    reads=(b_wdT, b_pq), writes=(b_pt,) if kcd == 0 else (), rw=(b_pt,) if kcd else ())
            xi, b_xi = xmin[mc % 2]
            if mc + 1 < FC:
                xi3, b_xi3 = xmin[(mc + 1) % 2]
                em.do("sync", lambda e, xi3=xi3, mc=mc, tok0=tok0: e.dma_start(
                    out=xi3[:], in_=xmid_h[mc + 1, :, tok0:tok0 + TT]),
                    deps=b_xmid_dram.deps_read(), writes=(b_xi3,), dma=f"d_xmin{(mc + 1) % 2}")
            ot, b_ot = otile[mc % 2]
            em.do("vector", lambda e, pt=pt, xi=xi, ot=ot, mc=mc, bb=bb: e.scalar_tensor_tensor(
                ot[:], pt[:], gb_ap(mc, bb), xi[:], op0=ALU.mult, op1=ALU.add),
                reads=(b_pt, b_modg, b_xi), writes=(b_ot,))
            tk = em.do("sync", lambda e, ot=ot, mc=mc, tok0=tok0: e.dma_start(
                out=out_h[mc, :, tok0:tok0 + TT], in_=ot[:]), reads=(b_ot,), dma=f"d_out{mc % 2}")
            out_toks.append(tk)

    if "ffn" in taps:
        tg_ = nc.dram_tensor("tap_g", [NTILES, MI, 128, TT], BF16, kind="ExternalOutput").ap()
        tu_ = nc.dram_tensor("tap_u", [NTILES, MI, 128, TT], BF16, kind="ExternalOutput").ap()
        tt_ = nc.dram_tensor("tap_t", [NTILES, MI, 128, TT], BF16, kind="ExternalOutput").ap()
        ts_ = nc.dram_tensor("tap_scl", [128, 12], F32, kind="ExternalOutput").ap()
        tq_ = nc.dram_tensor("tap_pq", [128, MI * TT], BF16, kind="ExternalOutput").ap()
        k1 = em.do("sync", lambda e: e.dma_start(out=tg_, in_=g_h), deps=b_g_dram.deps_read(), dma="d_tap")
        k2 = em.do("sync", lambda e: e.dma_start(out=tu_, in_=u_h), deps=b_u_dram.deps_read(), dma="d_tap")
        k3 = em.do("sync", lambda e: e.dma_start(out=tt_, in_=t_h), deps=b_t_dram.deps_read(), dma="d_tap")
        k4 = em.do("sync", lambda e: e.dma_start(out=ts_, in_=scl[:]), reads=(b_scl,), dma="d_tap")
        k5 = em.do("sync", lambda e: e.dma_start(out=tq_, in_=pqs[1][0][:]), reads=(pqs[1][1],), dma="d_tap")
        out_toks += [k1, k2, k3, k4, k5]

    em._push("sync", None, out_toks, None, 0)

    with nc.Block() as block:
        sems = {s_: ec(nc.semaphore(s_)) for s_ in em.counts}
        em.replay(block, sems)
    scopeC.close()
    st.close()
    return nc


def _host_consts():
    triA = np.triu(np.full((128, 128), -1.0 / 16.0, np.float32)).astype(BF)
    triR = np.tril(np.full((128, 128), -1.0 / 16.0, np.float32), -1).astype(BF)
    mask = np.triu(np.ones((128, 128), np.float32)).astype(BF)
    ones_ln = np.full((128, 128), 2.0 ** -10, np.float32).astype(BF)
    bd = np.zeros((128, 128), np.float32)
    bd[0:64, 0:64] = 2.0 ** -6
    bd[64:128, 64:128] = 2.0 ** -6
    bd_rms = bd.astype(BF)
    tri = np.concatenate([triA, triR, mask], axis=1)
    ones = np.concatenate([ones_ln, bd_rms], axis=1)
    kvec = np.tile(np.array([[0.0, EPS, LN8, 1.0]], np.float32), (128, 1))
    mask8 = np.tile(mask, (1, 8)).copy()
    onerow = np.ones((1, TT), np.float32).astype(BF)
    bone = np.ones((1, 128), np.float32)
    idn = np.eye(128, dtype=np.float32)
    return tri, ones, kvec, mask8, onerow, bone, idn


def kernel(**inputs):
    x = np.asarray(inputs["L0_attn_x"], np.float32)
    tri, ones, kvec, mask8, onerow, bone, idn = _host_consts()
    gnw = np.tile(np.asarray(inputs["L0_attn_g_norm_w"], np.float32), 2)[:, None].copy()
    wg2a = np.concatenate([np.asarray(inputs["L0_attn_Wg2"], np.float32),
                           np.asarray(inputs["L0_attn_bg2"], np.float32)[None, :]], axis=0)
    cfull = np.asarray(inputs["L0_c"], np.float32)

    nc = build(NCORES)
    in_maps = []
    for core in range(NCORES):
        bs = slice(core * BPC, (core + 1) * BPC)
        xs = x[bs]
        x_fm = np.ascontiguousarray(
            xs.transpose(2, 0, 1).reshape(FC, 128, BPC * L))
        ct = np.ascontiguousarray(
            cfull[bs].T.reshape(FC, 128, BPC))
        in_maps.append({
            "x": x_fm,
            "wq": np.asarray(inputs["L0_attn_Wq"], np.float32),
            "wk": np.asarray(inputs["L0_attn_Wk"], np.float32),
            "wv": np.asarray(inputs["L0_attn_Wv"], np.float32),
            "wgp": np.asarray(inputs["L0_attn_Wg"], np.float32),
            "wo": np.asarray(inputs["L0_attn_Wo"], np.float32),
            "wg1": np.asarray(inputs["L0_attn_Wg1"], np.float32),
            "wg2a": wg2a,
            "gnw": gnw,
            "wup": np.asarray(inputs["L0_ffn_w_up"], np.float32),
            "wgate": np.asarray(inputs["L0_ffn_w_gate"], np.float32),
            "wdown": np.asarray(inputs["L0_ffn_w_down"], np.float32),
            "ct": ct,
            "wmod": np.asarray(inputs["L0_wmod"], np.float32),
            "tri": tri, "onesc": ones, "kvec": kvec, "mask8": mask8, "onerow": onerow,
            "bone": bone, "idn": idn,
        })
    res = run_bass_kernel_spmd(nc, in_maps, core_ids=list(range(NCORES)))
    out = np.empty((B, L, DM), np.float32)
    for core in range(NCORES):
        o = res.results[core]["out"]
        o = o.reshape(DM, BPC, L).transpose(1, 2, 0)
        out[core * BPC:(core + 1) * BPC] = o
    return out
